# version 03
# baseline (speedup 1.0000x reference)
import os
import numpy as np
import ml_dtypes
import concourse.bass as bass
import concourse.mybir as mybir
from concourse.bass_utils import run_bass_kernel_spmd
from contextlib import ExitStack

F32 = mybir.dt.float32
BF16 = mybir.dt.bfloat16
I32 = mybir.dt.int32
AF = mybir.ActivationFunctionType
ALU = mybir.AluOpType
AXL = mybir.AxisListType

NCORES = 8
TOK = 2048
NT = 16
NS = 8
D = 1024
H = 16
WS = 8
SHIFT = 4

C_INV1024, C_EPS, C_NEGHALF, C_ONE, C_EIGHTH, C_WSF, C_127, C_NEGBIG, \
    C_NEG1, C_ZERO, C_INV127, C_NEG128 = range(12)

ST_MEAN = 0
ST_RSTD = 16
ST_ABS = 32
ST_RUNMAX = 40
ST_MX, ST_MN, ST_A, ST_B = 41, 42, 43, 44
ST_S1, ST_B2 = 45, 46
ST_TCOL, ST_NEGT, ST_RCOL, ST_WSIS = 47, 48, 49, 50
ST_T0 = 51
ST_L1 = 96
ST_L2 = 128
ST_RECIP = 64
ST_BQKQ = 80
ST_ABSTOT = 88
ST_WS = 89
ST_LMAX = 95
ST_GMAX = 91
ST_RSC = 92
ST_WSIS0 = 93
ST_BINV = 94


DBG_PHASE = int(os.environ.get("K_PHASE", "9"))
AUDIT = []
_DO_AUDIT = os.environ.get("K_AUDIT", "0") == "1"
DBG_SUPERS = int(os.environ.get("K_SUPERS", "8"))
DBG_STAGE = os.environ.get("K_STAGE", "full")


def build_nc():
    nc = bass.Bass("TRN2", target_bir_lowering=False,
                   detect_race_conditions=False)

    x_in = nc.dram_tensor("x_in", [TOK, D], F32, kind="ExternalInput")
    wqkv_in = nc.dram_tensor("wqkv_in", [D, 3 * D], F32, kind="ExternalInput")
    bqk_in = nc.dram_tensor("bqk_in", [128, 16], F32, kind="ExternalInput")
    bv_in = nc.dram_tensor("bv_in", [1, D], F32, kind="ExternalInput")
    wproj_in = nc.dram_tensor("wproj_in", [D, D], F32, kind="ExternalInput")
    bp_in = nc.dram_tensor("bp_in", [1, D], F32, kind="ExternalInput")
    wft_in = nc.dram_tensor("wft_in", [D, D], F32, kind="ExternalInput")
    bf_in = nc.dram_tensor("bf_in", [1, D], F32, kind="ExternalInput")
    eb_in = nc.dram_tensor("eb_in", [128, 2048], F32, kind="ExternalInput")
    consts_in = nc.dram_tensor("consts_in", [128, 16], F32, kind="ExternalInput")
    ones16_in = nc.dram_tensor("ones16_in", [128, 16], F32, kind="ExternalInput")
    ident_in = nc.dram_tensor("ident_in", [128, 128], F32, kind="ExternalInput")
    onesrow_in = nc.dram_tensor("onesrow_in", [1, 128], F32, kind="ExternalInput")

    y_out = nc.dram_tensor("y_out", [TOK, D], F32, kind="ExternalOutput")
    y_dbg = None
    if os.environ.get("K_DBG", "0") == "1":
        y_dbg = nc.dram_tensor("y_dbg", [128, 168], F32, kind="ExternalOutput")

    cc_in = nc.dram_tensor("cc_in", [128, 8], F32)
    cc_out = nc.dram_tensor("cc_out", [128, 8], F32)

    es = ExitStack()
    sb = lambda name, shape, dt: es.enter_context(nc.sbuf_tensor(name, shape, dt))
    ps = lambda name, shape, dt: es.enter_context(nc.psum_tensor(name, shape, dt))

    xres = sb("xres", [128, NT * D], F32)
    lnf = sb("lnf", [128, 2 * D], F32)
    scrA = sb("scrA", [128, D], BF16)
    scrB = sb("scrB", [128, D], BF16)
    lnT = sb("lnT", [128, 2048], BF16)
    qkT = sb("qkT", [128, 4096], BF16)
    vext = sb("vext", [128, 2 * 1040], BF16)
    pbuf = sb("pbuf", [128, 2048], BF16)
    eb = sb("eb", [128, 2048], BF16)
    obuf = sb("obuf", [128, D], F32)
    otb = sb("otb", [128, D], BF16)
    wqkv = sb("wqkv", [128, 8 * 3 * D], BF16)
    wproj = sb("wproj", [128, 8 * D], BF16)
    qwT = sb("qwT", [128, 8 * D], BF16)
    wfc = sb("wfc", [128, D], F32)
    mbuf = sb("mbuf", [128, D], BF16)
    bqk = sb("bqk", [128, 16], F32)
    consts = sb("consts", [128, 16], F32)
    ones16 = sb("ones16", [128, 16], F32)
    ident = sb("ident", [128, 128], F32)
    stat = sb("stat", [128, 168], F32)
    onesb = sb("onesb", [1, 128], BF16)
    onesf = sb("onesf", [1, 128], F32)
    brow = sb("brow", [1, 8], BF16)
    bvb = sb("bvb", [1, D], BF16)
    bpb = sb("bpb", [1, D], BF16)
    bfph = sb("bfph", [1, D], BF16)
    bfpl = sb("bfpl", [1, D], BF16)

    psQK = ps("psQK", [128, 1024], F32)
    psV = ps("psV", [128, 1024], F32)
    psS = ps("psS", [128, 2048], F32)

    sem_names = ["dX", "dW", "dST", "dCC", "CC", "sPE", "sV", "sS", "sG"]
    sems = {n: es.enter_context(nc.semaphore(n)) for n in sem_names}
    cnt = {n: 0 for n in sem_names}

    ops = {k: [] for k in ("sync", "gpsimd", "tensor", "vector", "scalar")}

    def _fence_op(eng):
        if eng == "vector":
            return lambda e: e.tensor_copy(stat[0:1, 160:161], consts[0:1, C_ONE:C_ONE + 1])
        else:
            return lambda e: e.activation(
                stat[0:1, 162:163], consts[0:1, C_ONE:C_ONE + 1], AF.Identity,
                bias=consts[0:1, C_ZERO:C_ZERO + 1],
                scale=consts[0:1, C_ONE:C_ONE + 1])

    SERIALIZE_VS = os.environ.get("K_SER", "1") == "1"
    last_m = {"vector": None, "scalar": None}

    def emit(eng, fn, inc=None, n=1, fence=True):
        if SERIALIZE_VS and eng in ("vector", "scalar"):
            sem = "sV" if eng == "vector" else "sS"
            if last_m[eng] is not None:
                wait(eng, sem, last_m[eng])
            if inc is None:
                inc, n, fence = sem, 1, False
            else:
                fence = False
            m = _emit_raw(eng, fn, inc, n, fence)
            if inc == sem:
                last_m[eng] = m
            else:
                cnt[sem] += 1
                last_m[eng] = cnt[sem]
                s2 = sems[sem]
                raise AssertionError("SERIALIZE_VS requires eng-own sem")
            return m
        return _emit_raw(eng, fn, inc, n, fence)

    def _emit_raw(eng, fn, inc=None, n=1, fence=True):
        m = None
        if inc is not None:
            cnt[inc] += n
            m = cnt[inc]
            s = sems[inc]
            use_fence = fence and eng in ("vector", "scalar")
            if _DO_AUDIT:
                def g(e, fn=fn, s=s, n=n, eng=eng, inc=inc, m=m, uf=use_fence):
                    if uf:
                        bi0 = fn(e)
                        AUDIT.append((eng, "inst", bi0.ins, None))
                        bi = _fence_op(eng)(e).then_inc(s, n)
                        AUDIT.append((eng, "inst", bi.ins, (inc, m, n)))
                    else:
                        bi = fn(e).then_inc(s, n)
                        AUDIT.append((eng, "inst", bi.ins, (inc, m, n)))
                    return bi
                ops[eng].append(g)
            else:
                if use_fence:
                    def g(e, fn=fn, s=s, n=n, eng=eng):
                        fn(e)
                        return _fence_op(eng)(e).then_inc(s, n)
                    ops[eng].append(g)
                else:
                    ops[eng].append(lambda e, fn=fn, s=s, n=n: fn(e).then_inc(s, n))
        else:
            if _DO_AUDIT:
                def g(e, fn=fn, eng=eng):
                    bi = fn(e)
                    AUDIT.append((eng, "inst", bi.ins, None))
                    return bi
                ops[eng].append(g)
            else:
                ops[eng].append(fn)
        return m

    def wait(eng, sem, val):
        if val is None or val <= 0:
            return
        s = sems[sem]
        if _DO_AUDIT:
            def g(e, s=s, v=val, eng=eng, sem=sem):
                AUDIT.append((eng, "wait", (sem, v), None))
                return e.wait_ge(s, v)
            ops[eng].append(g)
        else:
            ops[eng].append(lambda e, s=s, v=val: e.wait_ge(s, v))

    CN = lambda i: consts[:, i:i + 1]

    m_const = None
    for fn in (lambda e: e.dma_start(out=consts[:, :], in_=consts_in[:, :]),
               lambda e: e.dma_start(out=ones16[:, :], in_=ones16_in[:, :]),
               lambda e: e.dma_start(out=bqk[:, :], in_=bqk_in[:, :]),
               lambda e: e.dma_start(out=ident[:, :], in_=ident_in[:, :])):
        m_const = emit("sync", fn, inc="dX", n=16)
    m_onesf = emit("sync", lambda e: e.dma_start(out=onesf[:, :], in_=onesrow_in[:, :]), inc="dX", n=16)
    wait("scalar", "dX", m_const)
    m_x = [None] * NT
    for t in range(4):
        m_x[t] = emit("sync", lambda e, t=t: e.dma_start(
            out=xres[:, D * t:D * (t + 1)],
            in_=x_in[128 * t:128 * (t + 1), :]), inc="dX", n=16)
    stage_list = []
    for f in range(8):
        for h2 in range(2):
            stage_list.append((wqkv, 3072 * f + 1536 * h2, 1536,
                               wqkv_in, 128 * f, 1536 * h2))
    for f in range(8):
        stage_list.append((wproj, 1024 * f, 1024, wproj_in, 128 * f, 0))
    stagings = [lnf[:, 0:1536], qkT.bitcast(F32)[:, 0:1536]]
    m_conv_last = None
    m_stage_free = [None, None]
    for i, (dstT, dco, w, srcT, sro, sco) in enumerate(stage_list):
        sl = stagings[i % 2]
        wait("sync", "sS", m_stage_free[i % 2])
        mdma = emit("sync", lambda e, sl=sl, w=w, srcT=srcT, sro=sro, sco=sco: e.dma_start(
            out=sl[:, 0:w], in_=srcT[sro:sro + 128, sco:sco + w]), inc="dX", n=16)
        wait("scalar", "dX", mdma)
        m_conv_last = emit("scalar", lambda e, sl=sl, dstT=dstT, dco=dco, w=w: e.activation(
            dstT[:, dco:dco + w], sl[:, 0:w], AF.Identity,
            bias=CN(C_ZERO), scale=CN(C_ONE)), inc="sS")
        m_stage_free[i % 2] = m_conv_last
    m_wq = [m_conv_last] * 8
    m_wp = m_conv_last
    m_rows = None
    for srcT, dstT in ((bv_in, bvb), (bp_in, bpb)):
        wait("sync", "sV", m_rows)
        wait("sync", "sS", m_conv_last)
        mdma = emit("sync", lambda e, srcT=srcT: e.dma_start(
            out=qkT.bitcast(F32)[0:1, 0:1024], in_=srcT[0:1, :]), inc="dX", n=16)
        wait("vector", "dX", mdma)
        m_rows = emit("vector", lambda e, dstT=dstT: e.tensor_copy(
            dstT[0:1, :], qkT.bitcast(F32)[0:1, 0:1024]), inc="sV")
    wait("vector", "dX", m_onesf)
    m_rows = emit("vector", lambda e: e.tensor_copy(
        onesb[0:1, :], onesf[0:1, :]), inc="sV")

    m_ebp = [None] * 4
    for p in range(2):
        m_ebp[p] = emit("sync", lambda e, p=p: e.dma_start(
            out=lnf[:, 512 * p:512 * (p + 1)],
            in_=eb_in[:, 512 * p:512 * (p + 1)]), inc="dX", n=16)

    wait("vector", "dX", m_const)
    emit("vector", lambda e: e.tensor_copy(stat[:, ST_RUNMAX:ST_RUNMAX + 1], CN(C_NEGBIG)))
    for u in range(2):
        emit("vector", lambda e, u=u: e.tensor_copy(
            bass.AP(vext, 1040 * u + 64, [[2080, 128], [65, 16]]),
            ones16[:, :]))
    m_bqkq = emit("vector", lambda e: e.tensor_scalar(
        stat[:, ST_BQKQ:ST_BQKQ + 8], bqk[:, 0:8], CN(C_EIGHTH), None, ALU.mult),
        inc="sV")

    m_eb_done = None
    for p in range(2):
        wait("scalar", "dX", m_ebp[p])
        m_eb_done = emit("scalar", lambda e, p=p: e.activation(
            eb[:, 512 * p:512 * (p + 1)], lnf[:, 512 * p:512 * (p + 1)],
            AF.Exp, scale=CN(C_ONE)), inc="sS")
    for p in range(2, 4):
        wait("sync", "sS", m_eb_done if p == 2 else None)
        m_ebp[p] = emit("sync", lambda e, p=p: e.dma_start(
            out=lnf[:, 512 * (p - 2):512 * (p - 1)],
            in_=eb_in[:, 512 * p:512 * (p + 1)]), inc="dX", n=16)
    for p in range(2, 4):
        wait("scalar", "dX", m_ebp[p])
        m_eb_done = emit("scalar", lambda e, p=p: e.activation(
            eb[:, 512 * p:512 * (p + 1)], lnf[:, 512 * (p - 2):512 * (p - 1)],
            AF.Exp, scale=CN(C_ONE)), inc="sS")


    for t in range(4, NT):
        m_x[t] = emit("sync", lambda e, t=t: e.dma_start(
            out=xres[:, D * t:D * (t + 1)],
            in_=x_in[128 * t:128 * (t + 1), :]), inc="dX", n=16)

    m_apply = [None] * NT
    m_lnT_pe = [None] * NT
    ln1_emitted = [False] * NT

    def emit_ln1(t):
        if ln1_emitted[t]:
            return
        ln1_emitted[t] = True
        T0 = ST_L1 + 8 * (t % 4)
        xt = xres[:, D * t:D * (t + 1)]
        wait("scalar", "dX", m_x[t])
        mS = emit("scalar", lambda e, xt=xt: e.activation(
            scrB[:, :], xt, AF.Square), inc="sS")
        wait("vector", "dX", m_x[t])
        emit("vector", lambda e, xt=xt, T0=T0: e.tensor_reduce(
            stat[:, T0:T0 + 1], xt, AXL.X, ALU.add))
        wait("vector", "sS", mS)
        mrd = emit("vector", lambda e, T0=T0: e.tensor_reduce(
            stat[:, T0 + 1:T0 + 2], scrB[:, :], AXL.X, ALU.add), inc="sV",
            fence=False)
        wait("vector", "sV", mrd)
        emit("vector", lambda e, T0=T0: e.tensor_scalar(
            stat[:, T0 + 2:T0 + 3], stat[:, T0:T0 + 1],
            CN(C_INV1024), None, ALU.mult))
        emit("vector", lambda e, T0=T0: e.tensor_tensor(
            stat[:, T0 + 3:T0 + 4], stat[:, T0 + 2:T0 + 3],
            stat[:, T0 + 2:T0 + 3], ALU.mult))
        mV2 = emit("vector", lambda e, T0=T0: e.scalar_tensor_tensor(
            stat[:, T0 + 4:T0 + 5], stat[:, T0 + 1:T0 + 2],
            CN(C_INV1024), stat[:, T0 + 3:T0 + 4], ALU.mult, ALU.subtract),
            inc="sV")
        wait("scalar", "sV", mV2)
        emit("scalar", lambda e, T0=T0: e.activation(
            stat[:, T0 + 5:T0 + 6], stat[:, T0 + 4:T0 + 5],
            AF.Ln, bias=CN(C_EPS), scale=CN(C_ONE)))
        mS2 = emit("scalar", lambda e, T0=T0: e.activation(
            stat[:, T0 + 6:T0 + 7], stat[:, T0 + 5:T0 + 6],
            AF.Exp, scale=CN(C_NEGHALF)), inc="sS")
        wait("vector", "sS", mS2)
        mV3 = emit("vector", lambda e, T0=T0: e.tensor_scalar(
            stat[:, T0 + 7:T0 + 8], stat[:, T0 + 2:T0 + 3],
            stat[:, T0 + 6:T0 + 7], CN(C_NEG1), ALU.mult, ALU.mult),
            inc="sV")
        wait("scalar", "sPE", m_lnT_pe[t - 2] if t >= 2 else None)
        wait("scalar", "sV", mV3)
        m_apply[t] = emit("scalar", lambda e, t=t, xt=xt, T0=T0: e.activation(
            lnf[:, D * (t % 2):D * (t % 2 + 1)], xt, AF.Identity,
            bias=stat[:, T0 + 7:T0 + 8],
            scale=stat[:, T0 + 6:T0 + 7]), inc="sS")

    def emit_ln2(t):
        T0 = ST_L2 + 8 * (t % 4)
        xt = xres[:, D * t:D * (t + 1)]
        mV = emit("vector", lambda e, xt=xt, T0=T0: e.tensor_reduce(
            stat[:, T0:T0 + 1], xt, AXL.X, ALU.add), inc="sV", fence=False)
        wait("vector", "sV", mV)
        wait("scalar", "sV", mV)
        mS = emit("scalar", lambda e, xt=xt: e.activation(
            scrB[:, :], xt, AF.Square), inc="sS")
        emit("vector", lambda e, t=t, T0=T0: e.tensor_scalar(
            stat[:, ST_MEAN + t:ST_MEAN + t + 1], stat[:, T0:T0 + 1],
            CN(C_INV1024), None, ALU.mult))
        emit("vector", lambda e, t=t, T0=T0: e.tensor_tensor(
            stat[:, T0 + 2:T0 + 3], stat[:, ST_MEAN + t:ST_MEAN + t + 1],
            stat[:, ST_MEAN + t:ST_MEAN + t + 1], ALU.mult))
        wait("vector", "sS", mS)
        mrd2 = emit("vector", lambda e, T0=T0: e.tensor_reduce(
            stat[:, T0 + 1:T0 + 2], scrB[:, :], AXL.X, ALU.add), inc="sV",
            fence=False)
        wait("vector", "sV", mrd2)
        mV2 = emit("vector", lambda e, T0=T0: e.scalar_tensor_tensor(
            stat[:, T0 + 3:T0 + 4], stat[:, T0 + 1:T0 + 2],
            CN(C_INV1024), stat[:, T0 + 2:T0 + 3], ALU.mult, ALU.subtract),
            inc="sV")
        wait("scalar", "sV", mV2)
        emit("scalar", lambda e, T0=T0: e.activation(
            stat[:, T0 + 4:T0 + 5], stat[:, T0 + 3:T0 + 4],
            AF.Ln, bias=CN(C_EPS), scale=CN(C_ONE)))
        mS2 = emit("scalar", lambda e, t=t, T0=T0: e.activation(
            stat[:, ST_RSTD + t:ST_RSTD + t + 1], stat[:, T0 + 4:T0 + 5],
            AF.Exp, scale=CN(C_NEGHALF)), inc="sS")
        emit("vector", lambda e, xt=xt, T0=T0: e.tensor_reduce(
            stat[:, T0 + 5:T0 + 6], xt, AXL.X, ALU.max))
        mmx = emit("vector", lambda e, xt=xt, T0=T0: e.tensor_reduce(
            stat[:, T0 + 6:T0 + 7], xt, AXL.X, ALU.min), inc="sV",
            fence=False)
        wait("vector", "sV", mmx)
        wait("vector", "sS", mS2)
        emit("vector", lambda e, t=t, T0=T0: e.scalar_tensor_tensor(
            stat[:, T0 + 5:T0 + 6], stat[:, T0 + 5:T0 + 6], CN(C_ONE),
            stat[:, ST_MEAN + t:ST_MEAN + t + 1], ALU.mult, ALU.subtract))
        emit("vector", lambda e, t=t, T0=T0: e.scalar_tensor_tensor(
            stat[:, T0 + 6:T0 + 7], stat[:, ST_MEAN + t:ST_MEAN + t + 1],
            CN(C_ONE), stat[:, T0 + 6:T0 + 7], ALU.mult, ALU.subtract))
        emit("vector", lambda e, T0=T0: e.tensor_tensor(
            stat[:, T0 + 5:T0 + 6], stat[:, T0 + 5:T0 + 6],
            stat[:, T0 + 6:T0 + 7], ALU.max))
        emit("vector", lambda e, t=t, T0=T0: e.tensor_scalar(
            stat[:, T0 + 5:T0 + 6], stat[:, T0 + 5:T0 + 6],
            stat[:, ST_RSTD + t:ST_RSTD + t + 1], None, ALU.mult))
        return emit("vector", lambda e, T0=T0: e.tensor_tensor(
            stat[:, ST_RUNMAX:ST_RUNMAX + 1], stat[:, ST_RUNMAX:ST_RUNMAX + 1],
            stat[:, T0 + 5:T0 + 6], ALU.max), inc="sV")

    m_otb_evac = [None] * NT
    m_proj_evac = [None] * NT
    m_av_done = [None] * NT
    m_av_evac = [None] * NT
    m_scores = [None] * NT
    m_qk_evac_last = [None] * NS
    m_vmm = [[None, None] for _ in range(NS)]
    m_vevac = [[None, None] for _ in range(NS)]
    m_w1_abs = [None] * 8
    m_w1_dma = [None] * 8
    m_w2_dma = [None] * 8
    m_w2_done = [None] * 8
    m_thresh_bcast = None

    emit_ln1(0)
    emit_ln1(1)
    m_psqk_guard = None

    def _early_store(extra_waits):
        for sem, val in extra_waits:
            wait("sync", sem, val)
        tlist = list(range(NT))
        if os.environ.get("K_REV", "0") == "1":
            tlist = tlist[::-1]
        for t in tlist:
            emit("sync", lambda e, t=t: e.dma_start(
                out=y_out[128 * t:128 * (t + 1), :],
                in_=xres[:, D * t:D * (t + 1)]), inc="dST", n=16)
        wait("sync", "dST", cnt["dST"])

    if DBG_STAGE == "pro":
        _early_store([("sS", m_eb_done), ("dW", m_wq[7]), ("dX", m_x[15]),
                      ("sV", m_bqkq)])
        return _finish(nc, es, ops)

    for s in range(DBG_SUPERS):
        t0, t1 = 2 * s, 2 * s + 1

        for u, t in ((0, t0), (1, t1)):
            wait("tensor", "sS", m_apply[t])
            if u == 0:
                wait("tensor", "sS", m_otb_evac[t0 - 1] if t0 > 0 else None)
                if m_psqk_guard is not None:
                    wait("tensor", m_psqk_guard[0], m_psqk_guard[1])
                    m_psqk_guard = None
            else:
                wait("tensor", "sS", m_lnT_ev0)
            for j in range(8):
                last = j == 7
                m = emit("tensor", lambda e, j=j, t=t: e.transpose(
                    psQK[:, 128 * j:128 * (j + 1)],
                    lnf[:, D * (t % 2) + 128 * j:D * (t % 2) + 128 * (j + 1)],
                    ident[:, :]), inc="sPE" if last else None)
            m_lnT_pe[t] = m
            wait("scalar", "sPE", m_lnT_pe[t])
            mev = emit("scalar", lambda e, u=u: e.activation(
                bass.AP(lnT, 128 * u, [[2048, 128], [256, 8], [1, 128]]),
                psQK[:, :], AF.Identity, bias=CN(C_ZERO), scale=CN(C_ONE)),
                inc="sS")
            if u == 0:
                m_lnT_ev0 = mev
            else:
                m_lnT_ev1 = mev
        if s < NS - 1:
            emit_ln1(2 * s + 2)
            emit_ln1(2 * s + 3)
        if (s == DBG_SUPERS - 1) and DBG_SUPERS < NS and DBG_STAGE == "lnt":
            _early_store([("sS", m_lnT_ev1)])
            return _finish(nc, es, ops)

        m_qk_evac = [None] * 16
        NCH = int(os.environ.get("K_QKN", "16"))
        for c in range(NCH):
            slot_off = 0 if c % 2 == 0 else 512
            if c < 2:
                wait("tensor", "sS", m_lnT_ev1)
                wait("tensor", "sS", m_wq[7])
            else:
                prev = m_qk_evac[c - 2]
                wait("tensor", prev[1], prev[0])
            for f in range(8):
                mpe = emit("tensor", lambda e, c=c, f=f, so=slot_off: e.matmul(
                    psQK[:, so:so + 256],
                    wqkv[:, 3072 * f + 128 * c:3072 * f + 128 * (c + 1)],
                    lnT[:, 256 * f:256 * (f + 1)],
                    start=(f == 0), stop=(f == 7)),
                    inc="sPE" if f == 7 else None)
            if c < 8:
                wait("scalar", "sPE", mpe)
                wait("scalar", "sV", m_bqkq)
                mev = emit("scalar", lambda e, c=c, so=slot_off: e.activation(
                    qkT[:, 256 * c:256 * (c + 1)], psQK[:, so:so + 256],
                    AF.Identity, bias=stat[:, ST_BQKQ + c:ST_BQKQ + c + 1],
                    scale=CN(C_EIGHTH)), inc="sS")
                m_qk_evac[c] = (mev, "sS")
            elif os.environ.get("K_QKEV", "mix") == "s":
                wait("scalar", "sPE", mpe)
                mev = emit("scalar", lambda e, c=c, so=slot_off: e.activation(
                    qkT[:, 256 * c:256 * (c + 1)], psQK[:, so:so + 256],
                    AF.Identity, bias=bqk[:, c:c + 1], scale=CN(C_ONE)),
                    inc="sS")
                m_qk_evac[c] = (mev, "sS")
            else:
                wait("vector", "sPE", mpe)
                mev = emit("vector", lambda e, c=c, so=slot_off: e.tensor_scalar(
                    qkT[:, 256 * c:256 * (c + 1)], psQK[:, so:so + 256],
                    bqk[:, c:c + 1], None, ALU.add), inc="sV")
                m_qk_evac[c] = (mev, "sV")
        m_qk_evac_last[s] = m_qk_evac

        for u, t in ((0, t0), (1, t1)):
            if u == 0:
                wait("tensor", "sV", m_proj_evac[t0 - 1] if t0 > 0 else None)
                wait("tensor", "sV", m_rows)
            else:
                wait("tensor", "sV", m_vevac[s][0])
            wait("tensor", "sS", m_lnT_ev1 if u == 1 else m_lnT_ev0)
            for h2 in range(2):
                for f in range(8):
                    emit("tensor", lambda e, h2=h2, f=f, u=u: e.matmul(
                        psV[:, 512 * h2:512 * (h2 + 1)],
                        lnT[:, 256 * f + 128 * u:256 * f + 128 * (u + 1)],
                        wqkv[:, 3072 * f + 2048 + 512 * h2:3072 * f + 2048 + 512 * (h2 + 1)],
                        start=(f == 0), stop=False))
                m_vmm[s][u] = emit("tensor", lambda e, h2=h2: e.matmul(
                    psV[:, 512 * h2:512 * (h2 + 1)], onesb[:, :],
                    bvb[:, 512 * h2:512 * (h2 + 1)], start=False, stop=True),
                    inc="sPE" if h2 == 1 else None)
            wait("vector", "sPE", m_vmm[s][u])
            if s > 0:
                wait("vector", "sPE", m_av_done[t - 2])
            m_vevac[s][u] = emit("vector", lambda e, u=u: e.tensor_copy(
                bass.AP(vext, 1040 * u, [[2080, 128], [65, 16], [1, 64]]),
                bass.AP(psV, 0, [[1024, 128], [64, 16], [1, 64]])), inc="sV")

        last_super = (s == DBG_SUPERS - 1) and DBG_SUPERS < NS
        if last_super and DBG_STAGE == "qk":
            qk = m_qk_evac[NCH - 1]
            _early_store([(qk[1], qk[0])])
            return _finish(nc, es, ops)
        if last_super and DBG_STAGE == "v":
            _early_store([("sV", m_vevac[s][1])])
            return _finish(nc, es, ops)

        for u, t in ((0, t0), (1, t1)):
            wait("tensor", "sV", m_av_evac[t - 1] if t > 0 else None)
            qk = m_qk_evac[15]
            wait("tensor", qk[1], qk[0])
            wait("tensor", m_qk_evac[7][1], m_qk_evac[7][0])
            HORD = [0, 2, 4, 6, 8, 10, 12, 14, 1, 3, 5, 7, 9, 11, 13, 15]
            for hi, h in enumerate(HORD):
                base = 64 * (h % 2)
                kc = 256 * (8 + h // 2) + 128 * u
                qc = 256 * (h // 2) + 128 * u
                m_scores[t] = emit(
                    "tensor", lambda e, h=h, base=base, kc=kc, qc=qc: e.matmul(
                        psS[:, 128 * h:128 * (h + 1)],
                        qkT[base:base + 64, kc:kc + 128],
                        qkT[base:base + 64, qc:qc + 128],
                        start=True, stop=True),
                    inc="sPE" if hi == H - 1 else None)
            if last_super and DBG_STAGE == "scores" and u == 0:
                _early_store([("sPE", m_scores[t])])
                return _finish(nc, es, ops)
            wait("scalar", "sPE", m_scores[t])
            if t > 0:
                wait("scalar", "sPE", m_av_done[t - 1])
            wait("scalar", "sS", m_eb_done)
            mexp = emit("scalar", lambda e: e.activation(
                pbuf[:, :], psS[:, :], AF.Exp, scale=CN(C_ONE)), inc="sS")
            if last_super and DBG_STAGE == "exp" and u == 0:
                _early_store([("sS", mexp)])
                return _finish(nc, es, ops)
            wait("vector", "sS", mexp)
            meb = emit("vector", lambda e: e.tensor_tensor(
                pbuf[:, :], pbuf[:, :], eb[:, :], ALU.mult), inc="sV")
            if last_super and DBG_STAGE == "eb" and u == 0:
                _early_store([("sV", meb)])
                return _finish(nc, es, ops)
            wait("tensor", "sV", meb)
            wait("tensor", "sV", m_vevac[s][u])
            for h in range(H):
                m_av_done[t] = emit("tensor", lambda e, h=h, u=u: e.matmul(
                    psS[:, 128 * h:128 * h + 65],
                    pbuf[:, 128 * h:128 * (h + 1)],
                    vext[:, 1040 * u + 65 * h:1040 * u + 65 * (h + 1)],
                    start=True, stop=True),
                    inc="sPE" if h == H - 1 else None)
            if last_super and DBG_STAGE == "av" and u == 0:
                _early_store([("sPE", m_av_done[t])])
                return _finish(nc, es, ops)
            wait("vector", "sPE", m_av_done[t])
            emit("vector", lambda e: e.tensor_copy(
                obuf[:, :], bass.AP(psS, 0, [[2048, 128], [128, 16], [1, 64]])))
            mrcp = emit("vector", lambda e: e.reciprocal(
                stat[:, ST_RECIP:ST_RECIP + 16],
                bass.AP(psS, 64, [[2048, 128], [128, 16]])), inc="sV",
                fence=False)
            wait("vector", "sV", mrcp)
            for h in range(H):
                last = h == H - 1
                m_av_evac[t] = emit("vector", lambda e, h=h: e.tensor_scalar(
                    obuf[:, 64 * h:64 * (h + 1)], obuf[:, 64 * h:64 * (h + 1)],
                    stat[:, ST_RECIP + h:ST_RECIP + h + 1], None, ALU.mult),
                    inc="sV" if last else None)
            if last_super and DBG_STAGE == "avevac" and u == 0:
                wait("sync", "sV", m_av_evac[t])
                wait("gpsimd", "sV", m_av_evac[t])
                emit("sync", lambda e: e.dma_start(
                    out=y_out[0:128, :], in_=obuf[:, :]), inc="dST", n=16)
                emit("sync", lambda e: e.dma_start(
                    out=y_out[128:129, 0:16].rearrange("a b -> a b"),
                    in_=stat[0:1, ST_RECIP:ST_RECIP + 16]), inc="dST", n=16)
                emit("gpsimd", lambda e: e.dma_start(
                    out=y_out[256:384, 0:1024],
                    in_=pbuf[:, 0:1024]), inc="dST", n=16)
                emit("gpsimd", lambda e: e.dma_start(
                    out=y_out[384:512, 0:1024],
                    in_=pbuf[:, 1024:2048]), inc="dST", n=16)
                emit("gpsimd", lambda e: e.dma_start(
                    out=y_out[512:640, 0:1024],
                    in_=vext[:, 0:1024]), inc="dST", n=16)
                emit("gpsimd", lambda e: e.dma_start(
                    out=y_out[640:768, 0:1024],
                    in_=eb[:, 0:1024]), inc="dST", n=16)
                wait("sync", "dST", cnt["dST"])
                return _finish(nc, es, ops)
            wait("tensor", "sV", m_av_evac[t])
            if u == 1:
                wait("tensor", "sS", m_otb_evac[t0])
            for j in range(8):
                mot = emit("tensor", lambda e, j=j: e.transpose(
                    psQK[:, 128 * j:128 * (j + 1)],
                    obuf[:, 128 * j:128 * (j + 1)], ident[:, :]),
                    inc="sPE" if j == 7 else None)
            wait("scalar", "sPE", mot)
            m_otb_evac[t] = emit("scalar", lambda e: e.activation(
                otb[:, :], psQK[:, :], AF.Identity, bias=CN(C_ZERO),
                scale=CN(C_ONE)), inc="sS")
            if last_super and DBG_STAGE == "ot" and u == 0:
                _early_store([("sS", m_otb_evac[t])])
                return _finish(nc, es, ops)
            wait("tensor", "sS", m_otb_evac[t])
            wait("tensor", "sV", m_vevac[s][1])
            if u == 1:
                wait("tensor", "sV", m_proj_evac[t0])
            wait("tensor", "sS", m_wp)
            for h2 in range(2):
                for f in range(8):
                    emit("tensor", lambda e, h2=h2, f=f: e.matmul(
                        psV[:, 512 * h2:512 * (h2 + 1)],
                        otb[:, 128 * f:128 * (f + 1)],
                        wproj[:, 1024 * f + 512 * h2:1024 * f + 512 * (h2 + 1)],
                        start=(f == 0), stop=False))
                mproj = emit("tensor", lambda e, h2=h2: e.matmul(
                    psV[:, 512 * h2:512 * (h2 + 1)], onesb[:, :],
                    bpb[:, 512 * h2:512 * (h2 + 1)], start=False, stop=True),
                    inc="sPE" if h2 == 1 else None)
            wait("vector", "sPE", mproj)
            m_proj_evac[t] = emit("vector", lambda e, t=t: e.scalar_tensor_tensor(
                xres[:, D * t:D * (t + 1)], psV[:, :], CN(C_ONE),
                xres[:, D * t:D * (t + 1)], ALU.mult, ALU.add), inc="sV")
            if last_super and DBG_STAGE == "proj" and u == 0:
                _early_store([("sV", m_proj_evac[t])])
                return _finish(nc, es, ops)
            if os.environ.get("K_SNAP", "0") == "3":
                wait("sync", "sV", m_proj_evac[t])
                emit("sync", lambda e, t=t: e.dma_start(
                    out=y_out[128 * t:128 * (t + 1), :],
                    in_=xres[:, D * t:D * (t + 1)]), inc="dST", n=16)
            m_ln2_last = emit_ln2(t)

        if os.environ.get("K_NOW", "0") == "1":
            pass
        elif s < 4:
            for k in (2 * s, 2 * s + 1):
                wait("sync", "sV", m_w1_abs[k - 1] if k > 0 else None)
                m_w1_dma[k] = emit("sync", lambda e, k=k: e.dma_start(
                    out=wfc[:, :], in_=wft_in[128 * k:128 * (k + 1), :]),
                    inc="dX", n=16)
                wait("scalar", "dX", m_w1_dma[k])
                mwa = emit("scalar", lambda e, k=k: e.activation(
                    wfc[:, :], wfc[:, :], AF.Abs), inc="sS")
                wait("vector", "sS", mwa)
                m_w1_abs[k] = emit("vector", lambda e, k=k: e.tensor_reduce(
                    stat[:, ST_ABS + k:ST_ABS + k + 1], wfc[:, :],
                    AXL.X, ALU.add), inc="sV", fence=False)
        if os.environ.get("K_NOW", "0") in ("1", "2"):
            pass
        elif s == 3:
            KN = os.environ.get("K_NOW", "0")
            wait("vector", "sV", m_w1_abs[7])
            mta = emit("vector", lambda e: e.tensor_reduce(
                stat[:, ST_ABSTOT:ST_ABSTOT + 1], stat[:, ST_ABS:ST_ABS + 8],
                AXL.X, ALU.add), inc="sV")
            if KN != "4":
                wait("tensor", "sV", mta)
                wait("tensor", "sS", m_otb_evac[t1])
                mtr = emit("tensor", lambda e: e.transpose(
                    psQK[0:1, 0:128], stat[:, ST_ABSTOT:ST_ABSTOT + 1], ident[:, :]),
                    inc="sPE")
                wait("vector", "sPE", mtr)
                m_thr_read = emit("vector", lambda e: e.tensor_reduce(
                    stat[0:1, ST_WS:ST_WS + 1], psQK[0:1, 0:128], AXL.X, ALU.add),
                    inc="sV", fence=False)
                m_psqk_guard = ("sV", m_thr_read)
                wait("vector", "sV", m_thr_read)
                def vstep2(fn):
                    mm = emit("vector", fn, inc="sV", fence=False)
                    wait("vector", "sV", mm)
                    return mm
                mth = vstep2(lambda e: e.tensor_scalar(
                    stat[0:1, ST_WS + 1:ST_WS + 2], stat[0:1, ST_WS:ST_WS + 1],
                    CN(C_WSF)[0:1, :], None, ALU.mult))
            if KN not in ("3", "4"):
                vstep2(lambda e: e.tensor_copy(
                    brow[0:1, 0:1], stat[0:1, ST_WS + 1:ST_WS + 2]))
                vstep2(lambda e: e.tensor_tensor(
                    stat[0:1, ST_WS + 2:ST_WS + 3],
                    stat[0:1, ST_WS + 1:ST_WS + 2], brow[0:1, 0:1],
                    ALU.subtract))
                mdup = vstep2(lambda e: e.tensor_copy(
                    brow[0:1, 1:2], stat[0:1, ST_WS + 2:ST_WS + 3]))
                wait("tensor", "sV", mdup)
                emit("tensor", lambda e: e.matmul(
                    psQK[:, 512:513], onesb[0:1, :], brow[0:1, 0:1],
                    start=True, stop=False))
                mbmm = emit("tensor", lambda e: e.matmul(
                    psQK[:, 512:513], onesb[0:1, :], brow[0:1, 1:2],
                    start=False, stop=True), inc="sPE")
                wait("vector", "sPE", mbmm)
                emit("vector", lambda e: e.tensor_copy(
                    stat[:, ST_TCOL:ST_TCOL + 1], psQK[:, 512:513]))
                m_thresh_bcast = emit("vector", lambda e: e.tensor_scalar(
                    stat[:, ST_NEGT:ST_NEGT + 1], stat[:, ST_TCOL:ST_TCOL + 1],
                    CN(C_NEG1), None, ALU.mult), inc="sV")
                m_psqk_guard = ("sV", m_thresh_bcast)
        if os.environ.get("K_SNAP", "0") == "2" and s == 3:
            wait("sync", "sV", m_thresh_bcast)
            for rep in range(2):
                for c3 in range(5):
                    emit("sync", lambda e, c3=c3, rep=rep: e.dma_start(
                        out=y_out[640 * rep + 128 * c3:640 * rep + 128 * (c3 + 1), :],
                        in_=xres[:, 1024 * c3:1024 * (c3 + 1)]), inc="dST", n=16)
        if os.environ.get("K_SNAP", "0") == "1" and s <= 3:
            wait("sync", "sV", m_proj_evac[t1])
            for c3 in range(3):
                emit("sync", lambda e, s=s, c3=c3: e.dma_start(
                    out=y_out[512 * s + 128 * c3:512 * s + 128 * (c3 + 1), :],
                    in_=xres[:, 1024 * c3:1024 * (c3 + 1)]), inc="dST", n=16)
        if os.environ.get("K_NOW", "0") == "1":
            pass
        elif s >= 4:
            for k in (2 * (s - 4), 2 * (s - 4) + 1):
                wait("sync", "sV", m_thresh_bcast)
                wait("sync", "sV", m_w1_abs[7])
                if k > 0:
                    wait("sync", "sV", m_w2_done[k - 1])
                m_w2_dma[k] = emit("sync", lambda e, k=k: e.dma_start(
                    out=wfc[:, :], in_=wft_in[128 * k:128 * (k + 1), :]),
                    inc="dX", n=16)
                wait("vector", "dX", m_w2_dma[k])
                emit("vector", lambda e, k=k: e.tensor_scalar(
                    mbuf[:, :], wfc[:, :], stat[:, ST_TCOL:ST_TCOL + 1],
                    None, ALU.is_gt))
                emit("vector", lambda e, k=k: e.tensor_scalar(
                    qwT[:, D * k:D * (k + 1)], wfc[:, :],
                    stat[:, ST_NEGT:ST_NEGT + 1], None, ALU.is_lt))
                m_w2_done[k] = emit("vector", lambda e, k=k: e.tensor_tensor(
                    qwT[:, D * k:D * (k + 1)], mbuf[:, :],
                    qwT[:, D * k:D * (k + 1)], ALU.subtract), inc="sV")

    if DBG_SUPERS < NS:
        if os.environ.get("K_SNAP", "0") in ("1", "2"):
            wait("sync", "sV", m_proj_evac[2 * DBG_SUPERS - 1])
            wait("sync", "dST", cnt["dST"])
        else:
            _early_store([("sV", m_proj_evac[2 * DBG_SUPERS - 1])])
        return _finish(nc, es, ops)

    if DBG_PHASE <= 1:
        for t in range(NT):
            wait("sync", "sV", m_proj_evac[t])
            emit("sync", lambda e, t=t: e.dma_start(
                out=y_out[128 * t:128 * (t + 1), :],
                in_=xres[:, D * t:D * (t + 1)]), inc="dST", n=16)
        wait("sync", "dST", cnt["dST"])
        return _finish(nc, es, ops)

    if os.environ.get("K_SNAP", "0") == "3":
        wait("sync", "dST", cnt["dST"])
        return _finish(nc, es, ops)

    if os.environ.get("K_DUMP", "0") == "3":
        wait("sync", "sV", m_ln2_last)
        wait("sync", "sS", m_otb_evac[NT - 1])
        emit("sync", lambda e: e.dma_start(
            out=y_out[0:128, 0:168], in_=stat[:, :]), inc="dST", n=16)
        wait("sync", "dST", cnt["dST"])
        return _finish(nc, es, ops)

    wait("vector", "sV", m_ln2_last)
    mlm8 = emit("vector", lambda e: e.tensor_scalar(
        stat[:, ST_T0 + 4:ST_T0 + 12], ones16[:, 0:8],
        stat[:, ST_RUNMAX:ST_RUNMAX + 1], None, ALU.mult), inc="sV")
    wait("sync", "sV", mlm8)
    emit("sync", lambda e: e.dma_start(
        out=cc_in[:, :], in_=stat[:, ST_T0 + 4:ST_T0 + 12]), inc="dCC", n=16)
    wait("gpsimd", "dCC", cnt["dCC"])
    emit("gpsimd", lambda e: e.collective_compute(
        "AllReduce", ALU.max, replica_groups=[list(range(NCORES))],
        ins=[cc_in.ap().opt()], outs=[cc_out.ap().opt()]), inc="CC")
    wait("sync", "CC", cnt["CC"])
    mgm = emit("sync", lambda e: e.dma_start(
        out=stat[:, ST_T0 + 4:ST_T0 + 12], in_=cc_out[:, :]), inc="dCC", n=16)
    if os.environ.get("K_DUMP", "0") == "2":
        wait("sync", "dCC", mgm)
        emit("sync", lambda e: e.dma_start(
            out=y_out[0:128, 0:168], in_=stat[:, :]), inc="dST", n=16)
        wait("sync", "dST", cnt["dST"])
        return _finish(nc, es, ops)

    wait("tensor", "dCC", mgm)
    wait("tensor", "sS", m_otb_evac[NT - 1])
    mtr = emit("tensor", lambda e: e.transpose(
        psQK[0:1, 0:128], stat[:, ST_T0 + 4:ST_T0 + 5], ident[:, :]),
        inc="sPE")
    wait("vector", "sPE", mtr)
    mgr = emit("vector", lambda e: e.tensor_reduce(
        stat[0:1, ST_GMAX:ST_GMAX + 1], psQK[0:1, 0:128], AXL.X, ALU.max),
        inc="sV", fence=False)
    wait("vector", "sV", mgr)
    mgr2 = emit("vector", lambda e: e.reciprocal(
        stat[0:1, ST_T0:ST_T0 + 1], stat[0:1, ST_GMAX:ST_GMAX + 1]),
        inc="sV", fence=False)
    wait("vector", "sV", mgr2)
    def vstep(fn):
        mm = emit("vector", fn, inc="sV", fence=False)
        wait("vector", "sV", mm)
        return mm
    vstep(lambda e: e.tensor_scalar(
        stat[0:1, ST_RSC:ST_RSC + 1], stat[0:1, ST_T0:ST_T0 + 1],
        CN(C_127)[0:1, :], None, ALU.mult))
    vstep(lambda e: e.tensor_tensor(
        stat[0:1, ST_T0 + 2:ST_T0 + 3], stat[0:1, ST_WS + 1:ST_WS + 2],
        stat[0:1, ST_WS + 1:ST_WS + 2], ALU.add))
    vstep(lambda e: e.tensor_scalar(
        stat[0:1, ST_T0 + 3:ST_T0 + 4], stat[0:1, ST_GMAX:ST_GMAX + 1],
        CN(C_INV127)[0:1, :], None, ALU.mult))
    vstep(lambda e: e.tensor_tensor(
        stat[0:1, ST_WSIS0:ST_WSIS0 + 1], stat[0:1, ST_T0 + 2:ST_T0 + 3],
        stat[0:1, ST_T0 + 3:ST_T0 + 4], ALU.mult))
    mbi = vstep(lambda e: e.reciprocal(
        stat[0:1, ST_BINV:ST_BINV + 1], stat[0:1, ST_WSIS0:ST_WSIS0 + 1]))
    vstep(lambda e: e.tensor_copy(
        brow[0:1, 2:4], stat[0:1, ST_RSC:ST_RSC + 2]))
    vstep(lambda e: e.tensor_tensor(
        stat[0:1, ST_T0 + 2:ST_T0 + 4], stat[0:1, ST_RSC:ST_RSC + 2],
        brow[0:1, 2:4], ALU.subtract))
    mbi2 = vstep(lambda e: e.tensor_copy(
        brow[0:1, 4:6], stat[0:1, ST_T0 + 2:ST_T0 + 4]))
    wait("tensor", "sV", mbi2)
    emit("tensor", lambda e: e.matmul(
        psQK[:, 514:516], onesb[0:1, :], brow[0:1, 2:4],
        start=True, stop=False))
    mbmm2 = emit("tensor", lambda e: e.matmul(
        psQK[:, 514:516], onesb[0:1, :], brow[0:1, 4:6],
        start=False, stop=True), inc="sPE")
    wait("vector", "sPE", mbmm2)
    emit("vector", lambda e: e.tensor_copy(
        stat[:, ST_RCOL:ST_RCOL + 1], psQK[:, 514:515]))
    mbc = emit("vector", lambda e: e.tensor_copy(
        stat[:, ST_WSIS:ST_WSIS + 1], psQK[:, 515:516]), inc="sV")
    mbf = emit("sync", lambda e: e.dma_start(
        out=lnf[0:1, 0:1024], in_=bf_in[0:1, :]),
        inc="dX", n=16)
    wait("vector", "dX", mbf)
    wait("vector", "sV", mbi)
    vstep(lambda e: e.tensor_scalar(
        lnf[0:1, 0:1024], lnf[0:1, 0:1024],
        stat[0:1, ST_BINV:ST_BINV + 1], None, ALU.mult))
    vstep(lambda e: e.tensor_copy(
        bfph[0:1, :], lnf[0:1, 0:1024]))
    vstep(lambda e: e.tensor_tensor(
        lnf[0:1, 1024:2048], lnf[0:1, 0:1024],
        bfph[0:1, :], ALU.subtract))
    mrows2 = vstep(lambda e: e.tensor_copy(
        bfpl[0:1, :], lnf[0:1, 1024:2048]))

    if DBG_PHASE <= 2:
        wait("sync", "dCC", mbc)
        wait("sync", "sV", mrows2)
        for t in range(NT):
            wait("sync", "sV", m_proj_evac[t])
            emit("sync", lambda e, t=t: e.dma_start(
                out=y_out[128 * t:128 * (t + 1), :],
                in_=xres[:, D * t:D * (t + 1)]), inc="dST", n=16)
        wait("sync", "dST", cnt["dST"])
        return _finish(nc, es, ops)

    if os.environ.get("K_DUMP", "0") == "1":
        wait("sync", "sV", mbc)
        wait("sync", "sV", mrows2)
        emit("sync", lambda e: e.dma_start(
            out=y_out[0:128, 0:168], in_=stat[:, :]), inc="dST", n=16)
        wait("sync", "dST", cnt["dST"])
        return _finish(nc, es, ops)

    ST_S1S, ST_B2S = 96, 112
    wait("vector", "sV", mbc)
    wait("vector", "sV", mrows2)
    emit("vector", lambda e: e.tensor_scalar(
        stat[:, ST_S1S:ST_S1S + 16], stat[:, ST_RSTD:ST_RSTD + 16],
        stat[:, ST_RCOL:ST_RCOL + 1], None, ALU.mult))
    emit("vector", lambda e: e.tensor_scalar(
        stat[:, ST_B2S:ST_B2S + 16], stat[:, ST_MEAN:ST_MEAN + 16],
        CN(C_NEG1), None, ALU.mult))
    emit("vector", lambda e: e.tensor_tensor(
        stat[:, ST_B2S:ST_B2S + 16], stat[:, ST_B2S:ST_B2S + 16],
        stat[:, ST_S1S:ST_S1S + 16], ALU.mult))
    m_ffn_evac = [None] * NT
    m_qinT_evac = [None] * NT
    for t in range(NT):
        xt = xres[:, D * t:D * (t + 1)]
        slotA = lnf[:, 0:D]
        slotB = lnf[:, D:2 * D]
        if t > 0:
            wait("vector", "sS", m_qinT_evac[t - 1])
        emit("vector", lambda e, xt=xt, t=t: e.tensor_scalar(
            slotA, xt, stat[:, ST_S1S + t:ST_S1S + t + 1],
            stat[:, ST_B2S + t:ST_B2S + t + 1], ALU.mult, ALU.add))
        emit("vector", lambda e: e.tensor_scalar(
            slotA, slotA, CN(C_NEG128), CN(C_127), ALU.max, ALU.min))
        mi32 = emit("vector", lambda e: e.tensor_copy(
            obuf[:, :].bitcast(I32), slotA), inc="sV")
        wait("scalar", "sV", mi32)
        mqf2 = emit("scalar", lambda e: e.activation(
            slotB, obuf[:, :].bitcast(I32), AF.Identity,
            bias=CN(C_ZERO), scale=CN(C_ONE)), inc="sS")
        wait("tensor", "sS", mqf2)
        wait("tensor", "sS", m_qinT_evac[t - 1] if t > 0 else m_otb_evac[NT - 1])
        if t == 0:
            wait("tensor", "sV", mbc)
        for j in range(8):
            mqt = emit("tensor", lambda e, j=j, slotB=slotB: e.transpose(
                psQK[:, 128 * j:128 * (j + 1)], slotB[:, 128 * j:128 * (j + 1)],
                ident[:, :]), inc="sPE" if j == 7 else None)
        wait("scalar", "sPE", mqt)
        m_qinT_evac[t] = emit("scalar", lambda e: e.activation(
            otb[:, :], psQK[:, :], AF.Identity, bias=CN(C_ZERO),
            scale=CN(C_ONE)), inc="sS")
        wait("tensor", "sS", m_qinT_evac[t])
        wait("tensor", "sV", m_w2_done[7])
        wait("tensor", "sV", m_ffn_evac[t - 1] if t > 0 else m_proj_evac[NT - 1])
        for h2 in range(2):
            for f in range(8):
                emit("tensor", lambda e, h2=h2, f=f: e.matmul(
                    psV[:, 512 * h2:512 * (h2 + 1)],
                    otb[:, 128 * f:128 * (f + 1)],
                    qwT[:, 1024 * f + 512 * h2:1024 * f + 512 * (h2 + 1)],
                    start=(f == 0), stop=False))
            emit("tensor", lambda e, h2=h2: e.matmul(
                psV[:, 512 * h2:512 * (h2 + 1)], onesb[:, :],
                bfph[0:1, 512 * h2:512 * (h2 + 1)], start=False, stop=False))
            mffn = emit("tensor", lambda e, h2=h2: e.matmul(
                psV[:, 512 * h2:512 * (h2 + 1)], onesb[:, :],
                bfpl[0:1, 512 * h2:512 * (h2 + 1)], start=False, stop=True),
                inc="sPE" if h2 == 1 else None)
        wait("vector", "sPE", mffn)
        m_ffn_evac[t] = emit("vector", lambda e, xt=xt: e.scalar_tensor_tensor(
            xt, psV[:, :], stat[:, ST_WSIS:ST_WSIS + 1], xt,
            ALU.mult, ALU.add), inc="sV")
        if os.environ.get("K_DUMP", "0") == "4" and t == 0:
            wait("sync", "sV", m_ffn_evac[0])
            emit("sync", lambda e: e.dma_start(
                out=y_out[0:128, :], in_=xres[:, 0:D]), inc="dST", n=16)
            emit("sync", lambda e: e.dma_start(
                out=y_out[128:256, :], in_=lnf[:, D:2 * D]), inc="dST", n=16)
            emit("sync", lambda e: e.dma_start(
                out=y_out[256:384, :], in_=lnf[:, 0:D]), inc="dST", n=16)
            emit("sync", lambda e: e.dma_start(
                out=y_out[384:512, 0:168], in_=stat[:, :]), inc="dST", n=16)
            emit("sync", lambda e: e.dma_start(
                out=y_out[512:640, :].bitcast(BF16)[:, 0:1024],
                in_=otb[:, :]), inc="dST", n=16)
            emit("sync", lambda e: e.dma_start(
                out=y_out[640:768, :].bitcast(BF16)[:, 0:1024],
                in_=qwT[:, 0:1024]), inc="dST", n=16)
            wait("sync", "dST", cnt["dST"])
            return _finish(nc, es, ops)
        wait("sync", "sV", m_ffn_evac[t])
        emit("sync", lambda e, t=t: e.dma_start(
            out=y_out[128 * t:128 * (t + 1), :], in_=xres[:, D * t:D * (t + 1)]),
            inc="dST", n=16)

    if y_dbg is not None:
        wait("sync", "sV", cnt["sV"])
        emit("sync", lambda e: e.dma_start(out=y_dbg[:, :], in_=stat[:, :]),
             inc="dST", n=16)

    wait("sync", "dST", cnt["dST"])
    return _finish(nc, es, ops)


def _finish(nc, es, ops):
    with es, nc.Block() as block:
        @block.sync
        def _(e):
            for fn in ops["sync"]:
                fn(e)

        @block.gpsimd
        def _(e):
            for fn in ops["gpsimd"]:
                fn(e)

        @block.tensor
        def _(e):
            for fn in ops["tensor"]:
                fn(e)

        @block.vector
        def _(e):
            for fn in ops["vector"]:
                fn(e)

        @block.scalar
        def _(e):
            for fn in ops["scalar"]:
                fn(e)

    return nc


def _host_inputs(L0_attn_x, L0_attn_w_qkv, L0_attn_b_qkv, L0_attn_w_proj,
                 L0_attn_b_proj, L0_attn_bias_table, L0_ffn_weight, L0_ffn_bias):
    x = np.asarray(L0_attn_x, dtype=np.float32)
    wqkv = np.ascontiguousarray(np.asarray(L0_attn_w_qkv, dtype=np.float32))
    bqkv = np.asarray(L0_attn_b_qkv, dtype=np.float32)
    wproj = np.ascontiguousarray(np.asarray(L0_attn_w_proj, dtype=np.float32))
    bproj = np.asarray(L0_attn_b_proj, dtype=np.float32)
    table = np.asarray(L0_attn_bias_table, dtype=np.float32)
    wffn = np.asarray(L0_ffn_weight, dtype=np.float32)
    bffn = np.asarray(L0_ffn_bias, dtype=np.float32)

    B, N, C = x.shape
    xr = np.roll(x, -SHIFT, axis=1).reshape(B * N, C)
    shards = [np.ascontiguousarray(xr[c * TOK:(c + 1) * TOK]) for c in range(NCORES)]

    bqk_cols = np.ascontiguousarray(bqkv[:2048].reshape(16, 128).T)
    bv_row = np.ascontiguousarray(bqkv[2048:].reshape(1, 1024))
    bp_row = np.ascontiguousarray(bproj.reshape(1, 1024))
    wft = np.ascontiguousarray(wffn.T)
    bf_row = np.ascontiguousarray(bffn.reshape(1, 1024))

    kk, qq = np.meshgrid(np.arange(WS), np.arange(WS), indexing="ij")
    blk = table[112 + qq - kk]
    pat = np.full((128, H, 128), -1e9, dtype=np.float32)
    for w in range(16):
        pat[8 * w:8 * w + 8, :, 8 * w:8 * w + 8] = blk.transpose(0, 2, 1)
    ebias = np.ascontiguousarray(pat.reshape(128, H * 128))

    consts = np.zeros((128, 16), dtype=np.float32)
    vals = {C_INV1024: np.float32(1.0) / np.float32(1024.0),
            C_EPS: 1e-5, C_NEGHALF: -0.5, C_ONE: 1.0, C_EIGHTH: 0.125,
            C_WSF: np.float32(0.5) / np.float32(1048576.0),
            C_127: 127.0, C_NEGBIG: -1e30, C_NEG1: -1.0, C_ZERO: 0.0,
            C_INV127: np.float32(1.0) / np.float32(127.0), C_NEG128: -128.0}
    for i, v in vals.items():
        consts[:, i] = v
    ones16 = np.ones((128, 16), dtype=np.float32)
    ident = np.eye(128, dtype=np.float32)
    onesrow = np.ones((1, 128), dtype=np.float32)

    common = dict(wqkv_in=wqkv, bqk_in=bqk_cols, bv_in=bv_row,
                  wproj_in=wproj, bp_in=bp_row, wft_in=wft, bf_in=bf_row,
                  eb_in=ebias, consts_in=consts, ones16_in=ones16,
                  ident_in=ident, onesrow_in=onesrow)
    return shards, common, (B, N, C)


def kernel(L0_attn_x, L0_attn_w_qkv, L0_attn_b_qkv, L0_attn_w_proj,
           L0_attn_b_proj, L0_attn_bias_table, L0_ffn_weight, L0_ffn_bias):
    shards, common, (B, N, C) = _host_inputs(
        L0_attn_x, L0_attn_w_qkv, L0_attn_b_qkv, L0_attn_w_proj,
        L0_attn_b_proj, L0_attn_bias_table, L0_ffn_weight, L0_ffn_bias)
    nc = build_nc()
    in_maps = [dict(x_in=shards[c], **common) for c in range(NCORES)]
    res = run_bass_kernel_spmd(nc, in_maps, core_ids=list(range(NCORES)))
    out = np.concatenate([np.asarray(r["y_out"]) for r in res.results], axis=0)
    out = out.reshape(B, N, C)
    if os.environ.get("K_DBG", "0") == "1":
        kernel.dbg = [np.asarray(r["y_dbg"]) for r in res.results]
    return np.roll(out, SHIFT, axis=1)


# version 04
# speedup vs baseline: 1.0103x; 1.0103x over previous
import os
import numpy as np
import ml_dtypes
import concourse.bass as bass
import concourse.mybir as mybir
from concourse.bass_utils import run_bass_kernel_spmd
from contextlib import ExitStack

F32 = mybir.dt.float32
BF16 = mybir.dt.bfloat16
I32 = mybir.dt.int32
AF = mybir.ActivationFunctionType
ALU = mybir.AluOpType
AXL = mybir.AxisListType

NCORES = 8
TOK = 2048
NT = 16
NS = 8
D = 1024
H = 16
WS = 8
SHIFT = 4

C_INV1024, C_EPS, C_NEGHALF, C_ONE, C_EIGHTH, C_WSF, C_127, C_NEGBIG, \
    C_NEG1, C_ZERO, C_INV127, C_NEG128 = range(12)

ST_MEAN = 0
ST_RSTD = 16
ST_ABS = 32
ST_RUNMAX = 40
ST_MX, ST_MN, ST_A, ST_B = 41, 42, 43, 44
ST_S1, ST_B2 = 45, 46
ST_TCOL, ST_NEGT, ST_RCOL, ST_WSIS = 47, 48, 49, 50
ST_T0 = 51
ST_L1 = 96
ST_L2 = 128
ST_RECIP = 64
ST_BQKQ = 80
ST_ABSTOT = 88
ST_WS = 89
ST_LMAX = 95
ST_GMAX = 91
ST_RSC = 92
ST_WSIS0 = 93
ST_BINV = 94


DBG_PHASE = int(os.environ.get("K_PHASE", "9"))
AUDIT = []
_DO_AUDIT = os.environ.get("K_AUDIT", "0") == "1"
DBG_SUPERS = int(os.environ.get("K_SUPERS", "8"))
DBG_STAGE = os.environ.get("K_STAGE", "full")


def build_nc():
    nc = bass.Bass("TRN2", target_bir_lowering=False,
                   detect_race_conditions=False)

    x_in = nc.dram_tensor("x_in", [TOK, D], F32, kind="ExternalInput")
    wqkv_in = nc.dram_tensor("wqkv_in", [D, 3 * D], F32, kind="ExternalInput")
    bqk_in = nc.dram_tensor("bqk_in", [128, 16], F32, kind="ExternalInput")
    bv_in = nc.dram_tensor("bv_in", [1, D], F32, kind="ExternalInput")
    wproj_in = nc.dram_tensor("wproj_in", [D, D], F32, kind="ExternalInput")
    bp_in = nc.dram_tensor("bp_in", [1, D], F32, kind="ExternalInput")
    wft_in = nc.dram_tensor("wft_in", [D, D], F32, kind="ExternalInput")
    bf_in = nc.dram_tensor("bf_in", [1, D], F32, kind="ExternalInput")
    eb_in = nc.dram_tensor("eb_in", [128, 2048], F32, kind="ExternalInput")
    consts_in = nc.dram_tensor("consts_in", [128, 16], F32, kind="ExternalInput")
    ones16_in = nc.dram_tensor("ones16_in", [128, 16], F32, kind="ExternalInput")
    ident_in = nc.dram_tensor("ident_in", [128, 128], F32, kind="ExternalInput")
    onesrow_in = nc.dram_tensor("onesrow_in", [1, 128], F32, kind="ExternalInput")

    y_out = nc.dram_tensor("y_out", [TOK, D], F32, kind="ExternalOutput")
    y_dbg = None
    if os.environ.get("K_DBG", "0") == "1":
        y_dbg = nc.dram_tensor("y_dbg", [128, 168], F32, kind="ExternalOutput")

    cc_in = nc.dram_tensor("cc_in", [128, 8], F32)
    cc_out = nc.dram_tensor("cc_out", [128, 8], F32)

    es = ExitStack()
    sb = lambda name, shape, dt: es.enter_context(nc.sbuf_tensor(name, shape, dt))
    ps = lambda name, shape, dt: es.enter_context(nc.psum_tensor(name, shape, dt))

    xres = sb("xres", [128, NT * D], F32)
    lnf = sb("lnf", [128, 2 * D], F32)
    scrA = sb("scrA", [128, D], BF16)
    scrB = sb("scrB", [128, D], BF16)
    lnT = sb("lnT", [128, 2048], BF16)
    qkT = sb("qkT", [128, 4096], BF16)
    vext = sb("vext", [128, 2 * 1040], BF16)
    pbuf = sb("pbuf", [128, 2048], BF16)
    eb = sb("eb", [128, 2048], BF16)
    obuf = sb("obuf", [128, D], F32)
    otb = sb("otb", [128, D], BF16)
    wqkv = sb("wqkv", [128, 8 * 3 * D], BF16)
    wproj = sb("wproj", [128, 8 * D], BF16)
    qwT = sb("qwT", [128, 8 * D], BF16)
    wfc = sb("wfc", [128, D], F32)
    mbuf = sb("mbuf", [128, D], BF16)
    bqk = sb("bqk", [128, 16], F32)
    consts = sb("consts", [128, 16], F32)
    ones16 = sb("ones16", [128, 16], F32)
    ident = sb("ident", [128, 128], F32)
    stat = sb("stat", [128, 168], F32)
    onesb = sb("onesb", [1, 128], BF16)
    onesf = sb("onesf", [1, 128], F32)
    brow = sb("brow", [1, 8], BF16)
    bvb = sb("bvb", [1, D], BF16)
    bpb = sb("bpb", [1, D], BF16)
    bfph = sb("bfph", [1, D], BF16)
    bfpl = sb("bfpl", [1, D], BF16)

    psQK = ps("psQK", [128, 1024], F32)
    psV = ps("psV", [128, 1024], F32)
    psS = ps("psS", [128, 2048], F32)

    sem_names = ["dX", "dW", "dST", "dCC", "CC", "sPE", "sV", "sS", "sG", "dXE"]
    sems = {n: es.enter_context(nc.semaphore(n)) for n in sem_names}
    cnt = {n: 0 for n in sem_names}

    ops = {k: [] for k in ("sync", "gpsimd", "tensor", "vector", "scalar")}

    def _fence_op(eng):
        if eng == "vector":
            return lambda e: e.tensor_copy(stat[0:1, 160:161], consts[0:1, C_ONE:C_ONE + 1])
        else:
            return lambda e: e.activation(
                stat[0:1, 162:163], consts[0:1, C_ONE:C_ONE + 1], AF.Identity,
                bias=consts[0:1, C_ZERO:C_ZERO + 1],
                scale=consts[0:1, C_ONE:C_ONE + 1])

    SERIALIZE_VS = os.environ.get("K_SER", "1") == "1"
    last_m = {"vector": None, "scalar": None}

    def emit(eng, fn, inc=None, n=1, fence=True):
        if SERIALIZE_VS and eng in ("vector", "scalar"):
            sem = "sV" if eng == "vector" else "sS"
            if last_m[eng] is not None:
                wait(eng, sem, last_m[eng])
            if inc is None:
                inc, n, fence = sem, 1, False
            else:
                fence = False
            m = _emit_raw(eng, fn, inc, n, fence)
            if inc == sem:
                last_m[eng] = m
            else:
                cnt[sem] += 1
                last_m[eng] = cnt[sem]
                s2 = sems[sem]
                raise AssertionError("SERIALIZE_VS requires eng-own sem")
            return m
        return _emit_raw(eng, fn, inc, n, fence)

    def _emit_raw(eng, fn, inc=None, n=1, fence=True):
        m = None
        if inc is not None:
            cnt[inc] += n
            m = cnt[inc]
            s = sems[inc]
            use_fence = fence and eng in ("vector", "scalar")
            if _DO_AUDIT:
                def g(e, fn=fn, s=s, n=n, eng=eng, inc=inc, m=m, uf=use_fence):
                    if uf:
                        bi0 = fn(e)
                        AUDIT.append((eng, "inst", bi0.ins, None))
                        bi = _fence_op(eng)(e).then_inc(s, n)
                        AUDIT.append((eng, "inst", bi.ins, (inc, m, n)))
                    else:
                        bi = fn(e).then_inc(s, n)
                        AUDIT.append((eng, "inst", bi.ins, (inc, m, n)))
                    return bi
                ops[eng].append(g)
            else:
                if use_fence:
                    def g(e, fn=fn, s=s, n=n, eng=eng):
                        fn(e)
                        return _fence_op(eng)(e).then_inc(s, n)
                    ops[eng].append(g)
                else:
                    ops[eng].append(lambda e, fn=fn, s=s, n=n: fn(e).then_inc(s, n))
        else:
            if _DO_AUDIT:
                def g(e, fn=fn, eng=eng):
                    bi = fn(e)
                    AUDIT.append((eng, "inst", bi.ins, None))
                    return bi
                ops[eng].append(g)
            else:
                ops[eng].append(fn)
        return m

    def wait(eng, sem, val):
        if val is None or val <= 0:
            return
        s = sems[sem]
        if _DO_AUDIT:
            def g(e, s=s, v=val, eng=eng, sem=sem):
                AUDIT.append((eng, "wait", (sem, v), None))
                return e.wait_ge(s, v)
            ops[eng].append(g)
        else:
            ops[eng].append(lambda e, s=s, v=val: e.wait_ge(s, v))

    CN = lambda i: consts[:, i:i + 1]

    m_const = None
    for fn in (lambda e: e.dma_start(out=consts[:, :], in_=consts_in[:, :]),
               lambda e: e.dma_start(out=ones16[:, :], in_=ones16_in[:, :]),
               lambda e: e.dma_start(out=bqk[:, :], in_=bqk_in[:, :]),
               lambda e: e.dma_start(out=ident[:, :], in_=ident_in[:, :])):
        m_const = emit("sync", fn, inc="dX", n=16)
    m_onesf = emit("sync", lambda e: e.dma_start(out=onesf[:, :], in_=onesrow_in[:, :]), inc="dX", n=16)
    wait("scalar", "dX", m_const)
    m_x = [None] * NT
    for t in range(NT):
        m_x[t] = emit("sync", lambda e, t=t: e.dma_start(
            out=xres[:, D * t:D * (t + 1)],
            in_=x_in[128 * t:128 * (t + 1), :]), inc="dXE", n=16)
    stage_list = []
    for f in range(8):
        for h2 in range(2):
            stage_list.append((wqkv, 3072 * f + 1536 * h2, 1536,
                               wqkv_in, 128 * f, 1536 * h2))
    for f in range(8):
        stage_list.append((wproj, 1024 * f, 1024, wproj_in, 128 * f, 0))
    stagings = [lnf[:, 0:1536], qkT.bitcast(F32)[:, 0:1536]]
    m_conv_last = None
    m_stage_free = [None, None]
    for i, (dstT, dco, w, srcT, sro, sco) in enumerate(stage_list):
        sl = stagings[i % 2]
        wait("sync", "sS", m_stage_free[i % 2])
        mdma = emit("sync", lambda e, sl=sl, w=w, srcT=srcT, sro=sro, sco=sco: e.dma_start(
            out=sl[:, 0:w], in_=srcT[sro:sro + 128, sco:sco + w]), inc="dX", n=16)
        wait("scalar", "dX", mdma)
        m_conv_last = emit("scalar", lambda e, sl=sl, dstT=dstT, dco=dco, w=w: e.activation(
            dstT[:, dco:dco + w], sl[:, 0:w], AF.Identity,
            bias=CN(C_ZERO), scale=CN(C_ONE)), inc="sS")
        m_stage_free[i % 2] = m_conv_last
    m_wq = [m_conv_last] * 8
    m_wp = m_conv_last
    m_rows = None
    for srcT, dstT in ((bv_in, bvb), (bp_in, bpb)):
        wait("sync", "sV", m_rows)
        wait("sync", "sS", m_conv_last)
        mdma = emit("sync", lambda e, srcT=srcT: e.dma_start(
            out=qkT.bitcast(F32)[0:1, 0:1024], in_=srcT[0:1, :]), inc="dX", n=16)
        wait("vector", "dX", mdma)
        m_rows = emit("vector", lambda e, dstT=dstT: e.tensor_copy(
            dstT[0:1, :], qkT.bitcast(F32)[0:1, 0:1024]), inc="sV")
    wait("vector", "dX", m_onesf)
    m_rows = emit("vector", lambda e: e.tensor_copy(
        onesb[0:1, :], onesf[0:1, :]), inc="sV")

    m_ebp = [None] * 4
    for p in range(2):
        m_ebp[p] = emit("sync", lambda e, p=p: e.dma_start(
            out=lnf[:, 512 * p:512 * (p + 1)],
            in_=eb_in[:, 512 * p:512 * (p + 1)]), inc="dX", n=16)

    wait("vector", "dX", m_const)
    emit("vector", lambda e: e.tensor_copy(stat[:, ST_RUNMAX:ST_RUNMAX + 1], CN(C_NEGBIG)))
    for u in range(2):
        emit("vector", lambda e, u=u: e.tensor_copy(
            bass.AP(vext, 1040 * u + 64, [[2080, 128], [65, 16]]),
            ones16[:, :]))
    m_bqkq = emit("vector", lambda e: e.tensor_scalar(
        stat[:, ST_BQKQ:ST_BQKQ + 8], bqk[:, 0:8], CN(C_EIGHTH), None, ALU.mult),
        inc="sV")

    m_eb_done = None
    for p in range(2):
        wait("scalar", "dX", m_ebp[p])
        m_eb_done = emit("scalar", lambda e, p=p: e.activation(
            eb[:, 512 * p:512 * (p + 1)], lnf[:, 512 * p:512 * (p + 1)],
            AF.Exp, scale=CN(C_ONE)), inc="sS")
    for p in range(2, 4):
        wait("sync", "sS", m_eb_done if p == 2 else None)
        m_ebp[p] = emit("sync", lambda e, p=p: e.dma_start(
            out=lnf[:, 512 * (p - 2):512 * (p - 1)],
            in_=eb_in[:, 512 * p:512 * (p + 1)]), inc="dX", n=16)
    for p in range(2, 4):
        wait("scalar", "dX", m_ebp[p])
        m_eb_done = emit("scalar", lambda e, p=p: e.activation(
            eb[:, 512 * p:512 * (p + 1)], lnf[:, 512 * (p - 2):512 * (p - 1)],
            AF.Exp, scale=CN(C_ONE)), inc="sS")



    m_apply = [None] * NT
    m_lnT_pe = [None] * NT
    ln1_emitted = [False] * NT

    def emit_ln1(t):
        if ln1_emitted[t]:
            return
        ln1_emitted[t] = True
        T0 = ST_L1 + 8 * (t % 4)
        xt = xres[:, D * t:D * (t + 1)]
        wait("scalar", "dXE", 16 * NT)
        mS = emit("scalar", lambda e, xt=xt: e.activation(
            scrB[:, :], xt, AF.Square), inc="sS")
        wait("vector", "dXE", 16 * NT)
        emit("vector", lambda e, xt=xt, T0=T0: e.tensor_reduce(
            stat[:, T0:T0 + 1], xt, AXL.X, ALU.add))
        wait("vector", "sS", mS)
        mrd = emit("vector", lambda e, T0=T0: e.tensor_reduce(
            stat[:, T0 + 1:T0 + 2], scrB[:, :], AXL.X, ALU.add), inc="sV",
            fence=False)
        wait("vector", "sV", mrd)
        emit("vector", lambda e, T0=T0: e.tensor_scalar(
            stat[:, T0 + 2:T0 + 3], stat[:, T0:T0 + 1],
            CN(C_INV1024), None, ALU.mult))
        emit("vector", lambda e, T0=T0: e.tensor_tensor(
            stat[:, T0 + 3:T0 + 4], stat[:, T0 + 2:T0 + 3],
            stat[:, T0 + 2:T0 + 3], ALU.mult))
        mV2 = emit("vector", lambda e, T0=T0: e.scalar_tensor_tensor(
            stat[:, T0 + 4:T0 + 5], stat[:, T0 + 1:T0 + 2],
            CN(C_INV1024), stat[:, T0 + 3:T0 + 4], ALU.mult, ALU.subtract),
            inc="sV")
        wait("scalar", "sV", mV2)
        emit("scalar", lambda e, T0=T0: e.activation(
            stat[:, T0 + 5:T0 + 6], stat[:, T0 + 4:T0 + 5],
            AF.Ln, bias=CN(C_EPS), scale=CN(C_ONE)))
        mS2 = emit("scalar", lambda e, T0=T0: e.activation(
            stat[:, T0 + 6:T0 + 7], stat[:, T0 + 5:T0 + 6],
            AF.Exp, scale=CN(C_NEGHALF)), inc="sS")
        wait("vector", "sS", mS2)
        mV3 = emit("vector", lambda e, T0=T0: e.tensor_scalar(
            stat[:, T0 + 7:T0 + 8], stat[:, T0 + 2:T0 + 3],
            stat[:, T0 + 6:T0 + 7], CN(C_NEG1), ALU.mult, ALU.mult),
            inc="sV")
        wait("scalar", "sPE", m_lnT_pe[t - 2] if t >= 2 else None)
        wait("scalar", "sV", mV3)
        m_apply[t] = emit("scalar", lambda e, t=t, xt=xt, T0=T0: e.activation(
            lnf[:, D * (t % 2):D * (t % 2 + 1)], xt, AF.Identity,
            bias=stat[:, T0 + 7:T0 + 8],
            scale=stat[:, T0 + 6:T0 + 7]), inc="sS")

    def emit_ln2(t):
        T0 = ST_L2 + 8 * (t % 4)
        xt = xres[:, D * t:D * (t + 1)]
        mV = emit("vector", lambda e, xt=xt, T0=T0: e.tensor_reduce(
            stat[:, T0:T0 + 1], xt, AXL.X, ALU.add), inc="sV", fence=False)
        wait("vector", "sV", mV)
        wait("scalar", "sV", mV)
        mS = emit("scalar", lambda e, xt=xt: e.activation(
            scrB[:, :], xt, AF.Square), inc="sS")
        emit("vector", lambda e, t=t, T0=T0: e.tensor_scalar(
            stat[:, ST_MEAN + t:ST_MEAN + t + 1], stat[:, T0:T0 + 1],
            CN(C_INV1024), None, ALU.mult))
        emit("vector", lambda e, t=t, T0=T0: e.tensor_tensor(
            stat[:, T0 + 2:T0 + 3], stat[:, ST_MEAN + t:ST_MEAN + t + 1],
            stat[:, ST_MEAN + t:ST_MEAN + t + 1], ALU.mult))
        wait("vector", "sS", mS)
        mrd2 = emit("vector", lambda e, T0=T0: e.tensor_reduce(
            stat[:, T0 + 1:T0 + 2], scrB[:, :], AXL.X, ALU.add), inc="sV",
            fence=False)
        wait("vector", "sV", mrd2)
        mV2 = emit("vector", lambda e, T0=T0: e.scalar_tensor_tensor(
            stat[:, T0 + 3:T0 + 4], stat[:, T0 + 1:T0 + 2],
            CN(C_INV1024), stat[:, T0 + 2:T0 + 3], ALU.mult, ALU.subtract),
            inc="sV")
        wait("scalar", "sV", mV2)
        emit("scalar", lambda e, T0=T0: e.activation(
            stat[:, T0 + 4:T0 + 5], stat[:, T0 + 3:T0 + 4],
            AF.Ln, bias=CN(C_EPS), scale=CN(C_ONE)))
        mS2 = emit("scalar", lambda e, t=t, T0=T0: e.activation(
            stat[:, ST_RSTD + t:ST_RSTD + t + 1], stat[:, T0 + 4:T0 + 5],
            AF.Exp, scale=CN(C_NEGHALF)), inc="sS")
        emit("vector", lambda e, xt=xt, T0=T0: e.tensor_reduce(
            stat[:, T0 + 5:T0 + 6], xt, AXL.X, ALU.max))
        mmx = emit("vector", lambda e, xt=xt, T0=T0: e.tensor_reduce(
            stat[:, T0 + 6:T0 + 7], xt, AXL.X, ALU.min), inc="sV",
            fence=False)
        wait("vector", "sV", mmx)
        wait("vector", "sS", mS2)
        emit("vector", lambda e, t=t, T0=T0: e.scalar_tensor_tensor(
            stat[:, T0 + 5:T0 + 6], stat[:, T0 + 5:T0 + 6], CN(C_ONE),
            stat[:, ST_MEAN + t:ST_MEAN + t + 1], ALU.mult, ALU.subtract))
        emit("vector", lambda e, t=t, T0=T0: e.scalar_tensor_tensor(
            stat[:, T0 + 6:T0 + 7], stat[:, ST_MEAN + t:ST_MEAN + t + 1],
            CN(C_ONE), stat[:, T0 + 6:T0 + 7], ALU.mult, ALU.subtract))
        emit("vector", lambda e, T0=T0: e.tensor_tensor(
            stat[:, T0 + 5:T0 + 6], stat[:, T0 + 5:T0 + 6],
            stat[:, T0 + 6:T0 + 7], ALU.max))
        emit("vector", lambda e, t=t, T0=T0: e.tensor_scalar(
            stat[:, T0 + 5:T0 + 6], stat[:, T0 + 5:T0 + 6],
            stat[:, ST_RSTD + t:ST_RSTD + t + 1], None, ALU.mult))
        return emit("vector", lambda e, T0=T0: e.tensor_tensor(
            stat[:, ST_RUNMAX:ST_RUNMAX + 1], stat[:, ST_RUNMAX:ST_RUNMAX + 1],
            stat[:, T0 + 5:T0 + 6], ALU.max), inc="sV")

    m_otb_evac = [None] * NT
    m_proj_evac = [None] * NT
    m_av_done = [None] * NT
    m_av_evac = [None] * NT
    m_scores = [None] * NT
    m_qk_evac_last = [None] * NS
    m_vmm = [[None, None] for _ in range(NS)]
    m_vevac = [[None, None] for _ in range(NS)]
    m_w1_abs = [None] * 8
    m_w1_dma = [None] * 8
    m_w2_dma = [None] * 8
    m_w2_done = [None] * 8
    m_thresh_bcast = None

    emit_ln1(0)
    emit_ln1(1)
    m_psqk_guard = None

    def _early_store(extra_waits):
        for sem, val in extra_waits:
            wait("sync", sem, val)
        tlist = list(range(NT))
        if os.environ.get("K_REV", "0") == "1":
            tlist = tlist[::-1]
        for t in tlist:
            emit("sync", lambda e, t=t: e.dma_start(
                out=y_out[128 * t:128 * (t + 1), :],
                in_=xres[:, D * t:D * (t + 1)]), inc="dST", n=16)
        wait("sync", "dST", cnt["dST"])

    if DBG_STAGE == "pro":
        _early_store([("sS", m_eb_done), ("dW", m_wq[7]), ("dXE", 16 * NT),
                      ("sV", m_bqkq)])
        return _finish(nc, es, ops)

    for s in range(DBG_SUPERS):
        t0, t1 = 2 * s, 2 * s + 1

        for u, t in ((0, t0), (1, t1)):
            wait("tensor", "sS", m_apply[t])
            if u == 0:
                wait("tensor", "sS", m_otb_evac[t0 - 1] if t0 > 0 else None)
                if m_psqk_guard is not None:
                    wait("tensor", m_psqk_guard[0], m_psqk_guard[1])
                    m_psqk_guard = None
            else:
                wait("tensor", "sS", m_lnT_ev0)
            for j in range(8):
                last = j == 7
                m = emit("tensor", lambda e, j=j, t=t: e.transpose(
                    psQK[:, 128 * j:128 * (j + 1)],
                    lnf[:, D * (t % 2) + 128 * j:D * (t % 2) + 128 * (j + 1)],
                    ident[:, :]), inc="sPE" if last else None)
            m_lnT_pe[t] = m
            wait("scalar", "sPE", m_lnT_pe[t])
            mev = emit("scalar", lambda e, u=u: e.activation(
                bass.AP(lnT, 128 * u, [[2048, 128], [256, 8], [1, 128]]),
                psQK[:, :], AF.Identity, bias=CN(C_ZERO), scale=CN(C_ONE)),
                inc="sS")
            if u == 0:
                m_lnT_ev0 = mev
            else:
                m_lnT_ev1 = mev
        if s < NS - 1:
            emit_ln1(2 * s + 2)
            emit_ln1(2 * s + 3)
        if (s == DBG_SUPERS - 1) and DBG_SUPERS < NS and DBG_STAGE == "lnt":
            _early_store([("sS", m_lnT_ev1)])
            return _finish(nc, es, ops)

        m_qk_evac = [None] * 16
        NCH = int(os.environ.get("K_QKN", "16"))
        for c in range(NCH):
            slot_off = 0 if c % 2 == 0 else 512
            if c < 2:
                wait("tensor", "sS", m_lnT_ev1)
                wait("tensor", "sS", m_wq[7])
            else:
                prev = m_qk_evac[c - 2]
                wait("tensor", prev[1], prev[0])
            for f in range(8):
                mpe = emit("tensor", lambda e, c=c, f=f, so=slot_off: e.matmul(
                    psQK[:, so:so + 256],
                    wqkv[:, 3072 * f + 128 * c:3072 * f + 128 * (c + 1)],
                    lnT[:, 256 * f:256 * (f + 1)],
                    start=(f == 0), stop=(f == 7)),
                    inc="sPE" if f == 7 else None)
            if c < 8:
                wait("scalar", "sPE", mpe)
                wait("scalar", "sV", m_bqkq)
                mev = emit("scalar", lambda e, c=c, so=slot_off: e.activation(
                    qkT[:, 256 * c:256 * (c + 1)], psQK[:, so:so + 256],
                    AF.Identity, bias=stat[:, ST_BQKQ + c:ST_BQKQ + c + 1],
                    scale=CN(C_EIGHTH)), inc="sS")
                m_qk_evac[c] = (mev, "sS")
            elif os.environ.get("K_QKEV", "mix") == "s":
                wait("scalar", "sPE", mpe)
                mev = emit("scalar", lambda e, c=c, so=slot_off: e.activation(
                    qkT[:, 256 * c:256 * (c + 1)], psQK[:, so:so + 256],
                    AF.Identity, bias=bqk[:, c:c + 1], scale=CN(C_ONE)),
                    inc="sS")
                m_qk_evac[c] = (mev, "sS")
            else:
                wait("vector", "sPE", mpe)
                mev = emit("vector", lambda e, c=c, so=slot_off: e.tensor_scalar(
                    qkT[:, 256 * c:256 * (c + 1)], psQK[:, so:so + 256],
                    bqk[:, c:c + 1], None, ALU.add), inc="sV")
                m_qk_evac[c] = (mev, "sV")
        m_qk_evac_last[s] = m_qk_evac

        for u, t in ((0, t0), (1, t1)):
            if u == 0:
                wait("tensor", "sV", m_proj_evac[t0 - 1] if t0 > 0 else None)
                wait("tensor", "sV", m_rows)
            else:
                wait("tensor", "sV", m_vevac[s][0])
            wait("tensor", "sS", m_lnT_ev1 if u == 1 else m_lnT_ev0)
            for h2 in range(2):
                for f in range(8):
                    emit("tensor", lambda e, h2=h2, f=f, u=u: e.matmul(
                        psV[:, 512 * h2:512 * (h2 + 1)],
                        lnT[:, 256 * f + 128 * u:256 * f + 128 * (u + 1)],
                        wqkv[:, 3072 * f + 2048 + 512 * h2:3072 * f + 2048 + 512 * (h2 + 1)],
                        start=(f == 0), stop=False))
                m_vmm[s][u] = emit("tensor", lambda e, h2=h2: e.matmul(
                    psV[:, 512 * h2:512 * (h2 + 1)], onesb[:, :],
                    bvb[:, 512 * h2:512 * (h2 + 1)], start=False, stop=True),
                    inc="sPE" if h2 == 1 else None)
            wait("vector", "sPE", m_vmm[s][u])
            if s > 0:
                wait("vector", "sPE", m_av_done[t - 2])
            m_vevac[s][u] = emit("vector", lambda e, u=u: e.tensor_copy(
                bass.AP(vext, 1040 * u, [[2080, 128], [65, 16], [1, 64]]),
                bass.AP(psV, 0, [[1024, 128], [64, 16], [1, 64]])), inc="sV")

        last_super = (s == DBG_SUPERS - 1) and DBG_SUPERS < NS
        if last_super and DBG_STAGE == "qk":
            qk = m_qk_evac[NCH - 1]
            _early_store([(qk[1], qk[0])])
            return _finish(nc, es, ops)
        if last_super and DBG_STAGE == "v":
            _early_store([("sV", m_vevac[s][1])])
            return _finish(nc, es, ops)

        for u, t in ((0, t0), (1, t1)):
            wait("tensor", "sV", m_av_evac[t - 1] if t > 0 else None)
            qk = m_qk_evac[15]
            wait("tensor", qk[1], qk[0])
            wait("tensor", m_qk_evac[7][1], m_qk_evac[7][0])
            HORD = [0, 2, 4, 6, 8, 10, 12, 14, 1, 3, 5, 7, 9, 11, 13, 15]
            for hi, h in enumerate(HORD):
                base = 64 * (h % 2)
                kc = 256 * (8 + h // 2) + 128 * u
                qc = 256 * (h // 2) + 128 * u
                m_scores[t] = emit(
                    "tensor", lambda e, h=h, base=base, kc=kc, qc=qc: e.matmul(
                        psS[:, 128 * h:128 * (h + 1)],
                        qkT[base:base + 64, kc:kc + 128],
                        qkT[base:base + 64, qc:qc + 128],
                        start=True, stop=True),
                    inc="sPE" if hi == H - 1 else None)
            if last_super and DBG_STAGE == "scores" and u == 0:
                _early_store([("sPE", m_scores[t])])
                return _finish(nc, es, ops)
            wait("scalar", "sPE", m_scores[t])
            if t > 0:
                wait("scalar", "sPE", m_av_done[t - 1])
            wait("scalar", "sS", m_eb_done)
            mexp = emit("scalar", lambda e: e.activation(
                pbuf[:, :], psS[:, :], AF.Exp, scale=CN(C_ONE)), inc="sS")
            if last_super and DBG_STAGE == "exp" and u == 0:
                _early_store([("sS", mexp)])
                return _finish(nc, es, ops)
            wait("vector", "sS", mexp)
            meb = emit("vector", lambda e: e.tensor_tensor(
                pbuf[:, :], pbuf[:, :], eb[:, :], ALU.mult), inc="sV")
            if last_super and DBG_STAGE == "eb" and u == 0:
                _early_store([("sV", meb)])
                return _finish(nc, es, ops)
            wait("tensor", "sV", meb)
            wait("tensor", "sV", m_vevac[s][u])
            for h in range(H):
                m_av_done[t] = emit("tensor", lambda e, h=h, u=u: e.matmul(
                    psS[:, 128 * h:128 * h + 65],
                    pbuf[:, 128 * h:128 * (h + 1)],
                    vext[:, 1040 * u + 65 * h:1040 * u + 65 * (h + 1)],
                    start=True, stop=True),
                    inc="sPE" if h == H - 1 else None)
            if last_super and DBG_STAGE == "av" and u == 0:
                _early_store([("sPE", m_av_done[t])])
                return _finish(nc, es, ops)
            wait("vector", "sPE", m_av_done[t])
            emit("vector", lambda e: e.tensor_copy(
                obuf[:, :], bass.AP(psS, 0, [[2048, 128], [128, 16], [1, 64]])))
            mrcp = emit("vector", lambda e: e.reciprocal(
                stat[:, ST_RECIP:ST_RECIP + 16],
                bass.AP(psS, 64, [[2048, 128], [128, 16]])), inc="sV",
                fence=False)
            wait("vector", "sV", mrcp)
            for h in range(H):
                last = h == H - 1
                m_av_evac[t] = emit("vector", lambda e, h=h: e.tensor_scalar(
                    obuf[:, 64 * h:64 * (h + 1)], obuf[:, 64 * h:64 * (h + 1)],
                    stat[:, ST_RECIP + h:ST_RECIP + h + 1], None, ALU.mult),
                    inc="sV" if last else None)
            if last_super and DBG_STAGE == "avevac" and u == 0:
                wait("sync", "sV", m_av_evac[t])
                wait("gpsimd", "sV", m_av_evac[t])
                emit("sync", lambda e: e.dma_start(
                    out=y_out[0:128, :], in_=obuf[:, :]), inc="dST", n=16)
                emit("sync", lambda e: e.dma_start(
                    out=y_out[128:129, 0:16].rearrange("a b -> a b"),
                    in_=stat[0:1, ST_RECIP:ST_RECIP + 16]), inc="dST", n=16)
                emit("gpsimd", lambda e: e.dma_start(
                    out=y_out[256:384, 0:1024],
                    in_=pbuf[:, 0:1024]), inc="dST", n=16)
                emit("gpsimd", lambda e: e.dma_start(
                    out=y_out[384:512, 0:1024],
                    in_=pbuf[:, 1024:2048]), inc="dST", n=16)
                emit("gpsimd", lambda e: e.dma_start(
                    out=y_out[512:640, 0:1024],
                    in_=vext[:, 0:1024]), inc="dST", n=16)
                emit("gpsimd", lambda e: e.dma_start(
                    out=y_out[640:768, 0:1024],
                    in_=eb[:, 0:1024]), inc="dST", n=16)
                wait("sync", "dST", cnt["dST"])
                return _finish(nc, es, ops)
            wait("tensor", "sV", m_av_evac[t])
            if u == 1:
                wait("tensor", "sS", m_otb_evac[t0])
            for j in range(8):
                mot = emit("tensor", lambda e, j=j: e.transpose(
                    psQK[:, 128 * j:128 * (j + 1)],
                    obuf[:, 128 * j:128 * (j + 1)], ident[:, :]),
                    inc="sPE" if j == 7 else None)
            wait("scalar", "sPE", mot)
            m_otb_evac[t] = emit("scalar", lambda e: e.activation(
                otb[:, :], psQK[:, :], AF.Identity, bias=CN(C_ZERO),
                scale=CN(C_ONE)), inc="sS")
            if last_super and DBG_STAGE == "ot" and u == 0:
                _early_store([("sS", m_otb_evac[t])])
                return _finish(nc, es, ops)
            wait("tensor", "sS", m_otb_evac[t])
            wait("tensor", "sV", m_vevac[s][1])
            if u == 1:
                wait("tensor", "sV", m_proj_evac[t0])
            wait("tensor", "sS", m_wp)
            for h2 in range(2):
                for f in range(8):
                    emit("tensor", lambda e, h2=h2, f=f: e.matmul(
                        psV[:, 512 * h2:512 * (h2 + 1)],
                        otb[:, 128 * f:128 * (f + 1)],
                        wproj[:, 1024 * f + 512 * h2:1024 * f + 512 * (h2 + 1)],
                        start=(f == 0), stop=False))
                mproj = emit("tensor", lambda e, h2=h2: e.matmul(
                    psV[:, 512 * h2:512 * (h2 + 1)], onesb[:, :],
                    bpb[:, 512 * h2:512 * (h2 + 1)], start=False, stop=True),
                    inc="sPE" if h2 == 1 else None)
            wait("vector", "sPE", mproj)
            m_proj_evac[t] = emit("vector", lambda e, t=t: e.scalar_tensor_tensor(
                xres[:, D * t:D * (t + 1)], psV[:, :], CN(C_ONE),
                xres[:, D * t:D * (t + 1)], ALU.mult, ALU.add), inc="sV")
            if last_super and DBG_STAGE == "proj" and u == 0:
                _early_store([("sV", m_proj_evac[t])])
                return _finish(nc, es, ops)
            if os.environ.get("K_SNAP", "0") == "3":
                wait("sync", "sV", m_proj_evac[t])
                emit("sync", lambda e, t=t: e.dma_start(
                    out=y_out[128 * t:128 * (t + 1), :],
                    in_=xres[:, D * t:D * (t + 1)]), inc="dST", n=16)
            m_ln2_last = emit_ln2(t)

        if os.environ.get("K_NOW", "0") == "1":
            pass
        elif s < 4:
            for k in (2 * s, 2 * s + 1):
                wait("sync", "sV", m_w1_abs[k - 1] if k > 0 else None)
                m_w1_dma[k] = emit("sync", lambda e, k=k: e.dma_start(
                    out=wfc[:, :], in_=wft_in[128 * k:128 * (k + 1), :]),
                    inc="dX", n=16)
                wait("scalar", "dX", m_w1_dma[k])
                mwa = emit("scalar", lambda e, k=k: e.activation(
                    wfc[:, :], wfc[:, :], AF.Abs), inc="sS")
                wait("vector", "sS", mwa)
                m_w1_abs[k] = emit("vector", lambda e, k=k: e.tensor_reduce(
                    stat[:, ST_ABS + k:ST_ABS + k + 1], wfc[:, :],
                    AXL.X, ALU.add), inc="sV", fence=False)
        if os.environ.get("K_NOW", "0") in ("1", "2"):
            pass
        elif s == 3:
            KN = os.environ.get("K_NOW", "0")
            wait("vector", "sV", m_w1_abs[7])
            mta = emit("vector", lambda e: e.tensor_reduce(
                stat[:, ST_ABSTOT:ST_ABSTOT + 1], stat[:, ST_ABS:ST_ABS + 8],
                AXL.X, ALU.add), inc="sV")
            if KN != "4":
                wait("tensor", "sV", mta)
                wait("tensor", "sS", m_otb_evac[t1])
                mtr = emit("tensor", lambda e: e.transpose(
                    psQK[0:1, 0:128], stat[:, ST_ABSTOT:ST_ABSTOT + 1], ident[:, :]),
                    inc="sPE")
                wait("vector", "sPE", mtr)
                m_thr_read = emit("vector", lambda e: e.tensor_reduce(
                    stat[0:1, ST_WS:ST_WS + 1], psQK[0:1, 0:128], AXL.X, ALU.add),
                    inc="sV", fence=False)
                m_psqk_guard = ("sV", m_thr_read)
                wait("vector", "sV", m_thr_read)
                def vstep2(fn):
                    mm = emit("vector", fn, inc="sV", fence=False)
                    wait("vector", "sV", mm)
                    return mm
                mth = vstep2(lambda e: e.tensor_scalar(
                    stat[0:1, ST_WS + 1:ST_WS + 2], stat[0:1, ST_WS:ST_WS + 1],
                    CN(C_WSF)[0:1, :], None, ALU.mult))
            if KN not in ("3", "4"):
                vstep2(lambda e: e.tensor_copy(
                    brow[0:1, 0:1], stat[0:1, ST_WS + 1:ST_WS + 2]))
                vstep2(lambda e: e.tensor_tensor(
                    stat[0:1, ST_WS + 2:ST_WS + 3],
                    stat[0:1, ST_WS + 1:ST_WS + 2], brow[0:1, 0:1],
                    ALU.subtract))
                mdup = vstep2(lambda e: e.tensor_copy(
                    brow[0:1, 1:2], stat[0:1, ST_WS + 2:ST_WS + 3]))
                wait("tensor", "sV", mdup)
                emit("tensor", lambda e: e.matmul(
                    psQK[:, 512:513], onesb[0:1, :], brow[0:1, 0:1],
                    start=True, stop=False))
                mbmm = emit("tensor", lambda e: e.matmul(
                    psQK[:, 512:513], onesb[0:1, :], brow[0:1, 1:2],
                    start=False, stop=True), inc="sPE")
                wait("vector", "sPE", mbmm)
                emit("vector", lambda e: e.tensor_copy(
                    stat[:, ST_TCOL:ST_TCOL + 1], psQK[:, 512:513]))
                m_thresh_bcast = emit("vector", lambda e: e.tensor_scalar(
                    stat[:, ST_NEGT:ST_NEGT + 1], stat[:, ST_TCOL:ST_TCOL + 1],
                    CN(C_NEG1), None, ALU.mult), inc="sV")
                m_psqk_guard = ("sV", m_thresh_bcast)
        if os.environ.get("K_SNAP", "0") == "2" and s == 3:
            wait("sync", "sV", m_thresh_bcast)
            for rep in range(2):
                for c3 in range(5):
                    emit("sync", lambda e, c3=c3, rep=rep: e.dma_start(
                        out=y_out[640 * rep + 128 * c3:640 * rep + 128 * (c3 + 1), :],
                        in_=xres[:, 1024 * c3:1024 * (c3 + 1)]), inc="dST", n=16)
        if os.environ.get("K_SNAP", "0") == "1" and s <= 3:
            wait("sync", "sV", m_proj_evac[t1])
            for c3 in range(3):
                emit("sync", lambda e, s=s, c3=c3: e.dma_start(
                    out=y_out[512 * s + 128 * c3:512 * s + 128 * (c3 + 1), :],
                    in_=xres[:, 1024 * c3:1024 * (c3 + 1)]), inc="dST", n=16)
        if os.environ.get("K_NOW", "0") == "1":
            pass
        elif s >= 4:
            for k in (2 * (s - 4), 2 * (s - 4) + 1):
                wait("sync", "sV", m_thresh_bcast)
                wait("sync", "sV", m_w1_abs[7])
                if k > 0:
                    wait("sync", "sV", m_w2_done[k - 1])
                m_w2_dma[k] = emit("sync", lambda e, k=k: e.dma_start(
                    out=wfc[:, :], in_=wft_in[128 * k:128 * (k + 1), :]),
                    inc="dX", n=16)
                wait("vector", "dX", m_w2_dma[k])
                emit("vector", lambda e, k=k: e.tensor_scalar(
                    mbuf[:, :], wfc[:, :], stat[:, ST_TCOL:ST_TCOL + 1],
                    None, ALU.is_gt))
                emit("vector", lambda e, k=k: e.tensor_scalar(
                    qwT[:, D * k:D * (k + 1)], wfc[:, :],
                    stat[:, ST_NEGT:ST_NEGT + 1], None, ALU.is_lt))
                m_w2_done[k] = emit("vector", lambda e, k=k: e.tensor_tensor(
                    qwT[:, D * k:D * (k + 1)], mbuf[:, :],
                    qwT[:, D * k:D * (k + 1)], ALU.subtract), inc="sV")

    if DBG_SUPERS < NS:
        if os.environ.get("K_SNAP", "0") in ("1", "2"):
            wait("sync", "sV", m_proj_evac[2 * DBG_SUPERS - 1])
            wait("sync", "dST", cnt["dST"])
        else:
            _early_store([("sV", m_proj_evac[2 * DBG_SUPERS - 1])])
        return _finish(nc, es, ops)

    if DBG_PHASE <= 1:
        for t in range(NT):
            wait("sync", "sV", m_proj_evac[t])
            emit("sync", lambda e, t=t: e.dma_start(
                out=y_out[128 * t:128 * (t + 1), :],
                in_=xres[:, D * t:D * (t + 1)]), inc="dST", n=16)
        wait("sync", "dST", cnt["dST"])
        return _finish(nc, es, ops)

    if os.environ.get("K_SNAP", "0") == "3":
        wait("sync", "dST", cnt["dST"])
        return _finish(nc, es, ops)

    if os.environ.get("K_DUMP", "0") == "3":
        wait("sync", "sV", m_ln2_last)
        wait("sync", "sS", m_otb_evac[NT - 1])
        emit("sync", lambda e: e.dma_start(
            out=y_out[0:128, 0:168], in_=stat[:, :]), inc="dST", n=16)
        wait("sync", "dST", cnt["dST"])
        return _finish(nc, es, ops)

    wait("vector", "sV", m_ln2_last)
    mlm8 = emit("vector", lambda e: e.tensor_scalar(
        stat[:, ST_T0 + 4:ST_T0 + 12], ones16[:, 0:8],
        stat[:, ST_RUNMAX:ST_RUNMAX + 1], None, ALU.mult), inc="sV")
    wait("sync", "sV", mlm8)
    emit("sync", lambda e: e.dma_start(
        out=cc_in[:, :], in_=stat[:, ST_T0 + 4:ST_T0 + 12]), inc="dCC", n=16)
    wait("gpsimd", "dCC", cnt["dCC"])
    emit("gpsimd", lambda e: e.collective_compute(
        "AllReduce", ALU.max, replica_groups=[list(range(NCORES))],
        ins=[cc_in.ap().opt()], outs=[cc_out.ap().opt()]), inc="CC")
    wait("sync", "CC", cnt["CC"])
    mgm = emit("sync", lambda e: e.dma_start(
        out=stat[:, ST_T0 + 4:ST_T0 + 12], in_=cc_out[:, :]), inc="dCC", n=16)
    if os.environ.get("K_DUMP", "0") == "2":
        wait("sync", "dCC", mgm)
        emit("sync", lambda e: e.dma_start(
            out=y_out[0:128, 0:168], in_=stat[:, :]), inc="dST", n=16)
        wait("sync", "dST", cnt["dST"])
        return _finish(nc, es, ops)

    wait("tensor", "dCC", mgm)
    wait("tensor", "sS", m_otb_evac[NT - 1])
    mtr = emit("tensor", lambda e: e.transpose(
        psQK[0:1, 0:128], stat[:, ST_T0 + 4:ST_T0 + 5], ident[:, :]),
        inc="sPE")
    wait("vector", "sPE", mtr)
    mgr = emit("vector", lambda e: e.tensor_reduce(
        stat[0:1, ST_GMAX:ST_GMAX + 1], psQK[0:1, 0:128], AXL.X, ALU.max),
        inc="sV", fence=False)
    wait("vector", "sV", mgr)
    mgr2 = emit("vector", lambda e: e.reciprocal(
        stat[0:1, ST_T0:ST_T0 + 1], stat[0:1, ST_GMAX:ST_GMAX + 1]),
        inc="sV", fence=False)
    wait("vector", "sV", mgr2)
    def vstep(fn):
        mm = emit("vector", fn, inc="sV", fence=False)
        wait("vector", "sV", mm)
        return mm
    vstep(lambda e: e.tensor_scalar(
        stat[0:1, ST_RSC:ST_RSC + 1], stat[0:1, ST_T0:ST_T0 + 1],
        CN(C_127)[0:1, :], None, ALU.mult))
    vstep(lambda e: e.tensor_tensor(
        stat[0:1, ST_T0 + 2:ST_T0 + 3], stat[0:1, ST_WS + 1:ST_WS + 2],
        stat[0:1, ST_WS + 1:ST_WS + 2], ALU.add))
    vstep(lambda e: e.tensor_scalar(
        stat[0:1, ST_T0 + 3:ST_T0 + 4], stat[0:1, ST_GMAX:ST_GMAX + 1],
        CN(C_INV127)[0:1, :], None, ALU.mult))
    vstep(lambda e: e.tensor_tensor(
        stat[0:1, ST_WSIS0:ST_WSIS0 + 1], stat[0:1, ST_T0 + 2:ST_T0 + 3],
        stat[0:1, ST_T0 + 3:ST_T0 + 4], ALU.mult))
    mbi = vstep(lambda e: e.reciprocal(
        stat[0:1, ST_BINV:ST_BINV + 1], stat[0:1, ST_WSIS0:ST_WSIS0 + 1]))
    vstep(lambda e: e.tensor_copy(
        brow[0:1, 2:4], stat[0:1, ST_RSC:ST_RSC + 2]))
    vstep(lambda e: e.tensor_tensor(
        stat[0:1, ST_T0 + 2:ST_T0 + 4], stat[0:1, ST_RSC:ST_RSC + 2],
        brow[0:1, 2:4], ALU.subtract))
    mbi2 = vstep(lambda e: e.tensor_copy(
        brow[0:1, 4:6], stat[0:1, ST_T0 + 2:ST_T0 + 4]))
    wait("tensor", "sV", mbi2)
    emit("tensor", lambda e: e.matmul(
        psQK[:, 514:516], onesb[0:1, :], brow[0:1, 2:4],
        start=True, stop=False))
    mbmm2 = emit("tensor", lambda e: e.matmul(
        psQK[:, 514:516], onesb[0:1, :], brow[0:1, 4:6],
        start=False, stop=True), inc="sPE")
    wait("vector", "sPE", mbmm2)
    emit("vector", lambda e: e.tensor_copy(
        stat[:, ST_RCOL:ST_RCOL + 1], psQK[:, 514:515]))
    mbc = emit("vector", lambda e: e.tensor_copy(
        stat[:, ST_WSIS:ST_WSIS + 1], psQK[:, 515:516]), inc="sV")
    mbf = emit("sync", lambda e: e.dma_start(
        out=lnf[0:1, 0:1024], in_=bf_in[0:1, :]),
        inc="dX", n=16)
    wait("vector", "dX", mbf)
    wait("vector", "sV", mbi)
    vstep(lambda e: e.tensor_scalar(
        lnf[0:1, 0:1024], lnf[0:1, 0:1024],
        stat[0:1, ST_BINV:ST_BINV + 1], None, ALU.mult))
    vstep(lambda e: e.tensor_copy(
        bfph[0:1, :], lnf[0:1, 0:1024]))
    vstep(lambda e: e.tensor_tensor(
        lnf[0:1, 1024:2048], lnf[0:1, 0:1024],
        bfph[0:1, :], ALU.subtract))
    mrows2 = vstep(lambda e: e.tensor_copy(
        bfpl[0:1, :], lnf[0:1, 1024:2048]))

    if DBG_PHASE <= 2:
        wait("sync", "dCC", mbc)
        wait("sync", "sV", mrows2)
        for t in range(NT):
            wait("sync", "sV", m_proj_evac[t])
            emit("sync", lambda e, t=t: e.dma_start(
                out=y_out[128 * t:128 * (t + 1), :],
                in_=xres[:, D * t:D * (t + 1)]), inc="dST", n=16)
        wait("sync", "dST", cnt["dST"])
        return _finish(nc, es, ops)

    if os.environ.get("K_DUMP", "0") == "1":
        wait("sync", "sV", mbc)
        wait("sync", "sV", mrows2)
        emit("sync", lambda e: e.dma_start(
            out=y_out[0:128, 0:168], in_=stat[:, :]), inc="dST", n=16)
        wait("sync", "dST", cnt["dST"])
        return _finish(nc, es, ops)

    ST_S1S, ST_B2S = 96, 112
    wait("vector", "sV", mbc)
    wait("vector", "sV", mrows2)
    emit("vector", lambda e: e.tensor_scalar(
        stat[:, ST_S1S:ST_S1S + 16], stat[:, ST_RSTD:ST_RSTD + 16],
        stat[:, ST_RCOL:ST_RCOL + 1], None, ALU.mult))
    emit("vector", lambda e: e.tensor_scalar(
        stat[:, ST_B2S:ST_B2S + 16], stat[:, ST_MEAN:ST_MEAN + 16],
        CN(C_NEG1), None, ALU.mult))
    emit("vector", lambda e: e.tensor_tensor(
        stat[:, ST_B2S:ST_B2S + 16], stat[:, ST_B2S:ST_B2S + 16],
        stat[:, ST_S1S:ST_S1S + 16], ALU.mult))
    m_ffn_evac = [None] * NT
    m_qinT_evac = [None] * NT
    for t in range(NT):
        xt = xres[:, D * t:D * (t + 1)]
        slotA = lnf[:, 0:D]
        slotB = lnf[:, D:2 * D]
        if t > 0:
            wait("vector", "sS", m_qinT_evac[t - 1])
        emit("vector", lambda e, xt=xt, t=t: e.tensor_scalar(
            slotA, xt, stat[:, ST_S1S + t:ST_S1S + t + 1],
            stat[:, ST_B2S + t:ST_B2S + t + 1], ALU.mult, ALU.add))
        emit("vector", lambda e: e.tensor_scalar(
            slotA, slotA, CN(C_NEG128), CN(C_127), ALU.max, ALU.min))
        mi32 = emit("vector", lambda e: e.tensor_copy(
            obuf[:, :].bitcast(I32), slotA), inc="sV")
        wait("scalar", "sV", mi32)
        mqf2 = emit("scalar", lambda e: e.activation(
            slotB, obuf[:, :].bitcast(I32), AF.Identity,
            bias=CN(C_ZERO), scale=CN(C_ONE)), inc="sS")
        wait("tensor", "sS", mqf2)
        wait("tensor", "sS", m_qinT_evac[t - 1] if t > 0 else m_otb_evac[NT - 1])
        if t == 0:
            wait("tensor", "sV", mbc)
        for j in range(8):
            mqt = emit("tensor", lambda e, j=j, slotB=slotB: e.transpose(
                psQK[:, 128 * j:128 * (j + 1)], slotB[:, 128 * j:128 * (j + 1)],
                ident[:, :]), inc="sPE" if j == 7 else None)
        wait("scalar", "sPE", mqt)
        m_qinT_evac[t] = emit("scalar", lambda e: e.activation(
            otb[:, :], psQK[:, :], AF.Identity, bias=CN(C_ZERO),
            scale=CN(C_ONE)), inc="sS")
        wait("tensor", "sS", m_qinT_evac[t])
        wait("tensor", "sV", m_w2_done[7])
        wait("tensor", "sV", m_ffn_evac[t - 1] if t > 0 else m_proj_evac[NT - 1])
        for h2 in range(2):
            for f in range(8):
                emit("tensor", lambda e, h2=h2, f=f: e.matmul(
                    psV[:, 512 * h2:512 * (h2 + 1)],
                    otb[:, 128 * f:128 * (f + 1)],
                    qwT[:, 1024 * f + 512 * h2:1024 * f + 512 * (h2 + 1)],
                    start=(f == 0), stop=False))
            emit("tensor", lambda e, h2=h2: e.matmul(
                psV[:, 512 * h2:512 * (h2 + 1)], onesb[:, :],
                bfph[0:1, 512 * h2:512 * (h2 + 1)], start=False, stop=False))
            mffn = emit("tensor", lambda e, h2=h2: e.matmul(
                psV[:, 512 * h2:512 * (h2 + 1)], onesb[:, :],
                bfpl[0:1, 512 * h2:512 * (h2 + 1)], start=False, stop=True),
                inc="sPE" if h2 == 1 else None)
        wait("vector", "sPE", mffn)
        m_ffn_evac[t] = emit("vector", lambda e, xt=xt: e.scalar_tensor_tensor(
            xt, psV[:, :], stat[:, ST_WSIS:ST_WSIS + 1], xt,
            ALU.mult, ALU.add), inc="sV")
        if os.environ.get("K_DUMP", "0") == "4" and t == 0:
            wait("sync", "sV", m_ffn_evac[0])
            emit("sync", lambda e: e.dma_start(
                out=y_out[0:128, :], in_=xres[:, 0:D]), inc="dST", n=16)
            emit("sync", lambda e: e.dma_start(
                out=y_out[128:256, :], in_=lnf[:, D:2 * D]), inc="dST", n=16)
            emit("sync", lambda e: e.dma_start(
                out=y_out[256:384, :], in_=lnf[:, 0:D]), inc="dST", n=16)
            emit("sync", lambda e: e.dma_start(
                out=y_out[384:512, 0:168], in_=stat[:, :]), inc="dST", n=16)
            emit("sync", lambda e: e.dma_start(
                out=y_out[512:640, :].bitcast(BF16)[:, 0:1024],
                in_=otb[:, :]), inc="dST", n=16)
            emit("sync", lambda e: e.dma_start(
                out=y_out[640:768, :].bitcast(BF16)[:, 0:1024],
                in_=qwT[:, 0:1024]), inc="dST", n=16)
            wait("sync", "dST", cnt["dST"])
            return _finish(nc, es, ops)
        wait("sync", "sV", m_ffn_evac[t])
        emit("sync", lambda e, t=t: e.dma_start(
            out=y_out[128 * t:128 * (t + 1), :], in_=xres[:, D * t:D * (t + 1)]),
            inc="dST", n=16)

    if y_dbg is not None:
        wait("sync", "sV", cnt["sV"])
        emit("sync", lambda e: e.dma_start(out=y_dbg[:, :], in_=stat[:, :]),
             inc="dST", n=16)

    wait("sync", "dST", cnt["dST"])
    return _finish(nc, es, ops)


def _finish(nc, es, ops):
    with es, nc.Block() as block:
        @block.sync
        def _(e):
            for fn in ops["sync"]:
                fn(e)

        @block.gpsimd
        def _(e):
            for fn in ops["gpsimd"]:
                fn(e)

        @block.tensor
        def _(e):
            for fn in ops["tensor"]:
                fn(e)

        @block.vector
        def _(e):
            for fn in ops["vector"]:
                fn(e)

        @block.scalar
        def _(e):
            for fn in ops["scalar"]:
                fn(e)

    return nc


def _host_inputs(L0_attn_x, L0_attn_w_qkv, L0_attn_b_qkv, L0_attn_w_proj,
                 L0_attn_b_proj, L0_attn_bias_table, L0_ffn_weight, L0_ffn_bias):
    x = np.asarray(L0_attn_x, dtype=np.float32)
    wqkv = np.ascontiguousarray(np.asarray(L0_attn_w_qkv, dtype=np.float32))
    bqkv = np.asarray(L0_attn_b_qkv, dtype=np.float32)
    wproj = np.ascontiguousarray(np.asarray(L0_attn_w_proj, dtype=np.float32))
    bproj = np.asarray(L0_attn_b_proj, dtype=np.float32)
    table = np.asarray(L0_attn_bias_table, dtype=np.float32)
    wffn = np.asarray(L0_ffn_weight, dtype=np.float32)
    bffn = np.asarray(L0_ffn_bias, dtype=np.float32)

    B, N, C = x.shape
    xr = np.roll(x, -SHIFT, axis=1).reshape(B * N, C)
    shards = [np.ascontiguousarray(xr[c * TOK:(c + 1) * TOK]) for c in range(NCORES)]

    bqk_cols = np.ascontiguousarray(bqkv[:2048].reshape(16, 128).T)
    bv_row = np.ascontiguousarray(bqkv[2048:].reshape(1, 1024))
    bp_row = np.ascontiguousarray(bproj.reshape(1, 1024))
    wft = np.ascontiguousarray(wffn.T)
    bf_row = np.ascontiguousarray(bffn.reshape(1, 1024))

    kk, qq = np.meshgrid(np.arange(WS), np.arange(WS), indexing="ij")
    blk = table[112 + qq - kk]
    pat = np.full((128, H, 128), -1e9, dtype=np.float32)
    for w in range(16):
        pat[8 * w:8 * w + 8, :, 8 * w:8 * w + 8] = blk.transpose(0, 2, 1)
    ebias = np.ascontiguousarray(pat.reshape(128, H * 128))

    consts = np.zeros((128, 16), dtype=np.float32)
    vals = {C_INV1024: np.float32(1.0) / np.float32(1024.0),
            C_EPS: 1e-5, C_NEGHALF: -0.5, C_ONE: 1.0, C_EIGHTH: 0.125,
            C_WSF: np.float32(0.5) / np.float32(1048576.0),
            C_127: 127.0, C_NEGBIG: -1e30, C_NEG1: -1.0, C_ZERO: 0.0,
            C_INV127: np.float32(1.0) / np.float32(127.0), C_NEG128: -128.0}
    for i, v in vals.items():
        consts[:, i] = v
    ones16 = np.ones((128, 16), dtype=np.float32)
    ident = np.eye(128, dtype=np.float32)
    onesrow = np.ones((1, 128), dtype=np.float32)

    common = dict(wqkv_in=wqkv, bqk_in=bqk_cols, bv_in=bv_row,
                  wproj_in=wproj, bp_in=bp_row, wft_in=wft, bf_in=bf_row,
                  eb_in=ebias, consts_in=consts, ones16_in=ones16,
                  ident_in=ident, onesrow_in=onesrow)
    return shards, common, (B, N, C)


def kernel(L0_attn_x, L0_attn_w_qkv, L0_attn_b_qkv, L0_attn_w_proj,
           L0_attn_b_proj, L0_attn_bias_table, L0_ffn_weight, L0_ffn_bias):
    shards, common, (B, N, C) = _host_inputs(
        L0_attn_x, L0_attn_w_qkv, L0_attn_b_qkv, L0_attn_w_proj,
        L0_attn_b_proj, L0_attn_bias_table, L0_ffn_weight, L0_ffn_bias)
    nc = build_nc()
    in_maps = [dict(x_in=shards[c], **common) for c in range(NCORES)]
    res = run_bass_kernel_spmd(nc, in_maps, core_ids=list(range(NCORES)))
    out = np.concatenate([np.asarray(r["y_out"]) for r in res.results], axis=0)
    out = out.reshape(B, N, C)
    if os.environ.get("K_DBG", "0") == "1":
        kernel.dbg = [np.asarray(r["y_dbg"]) for r in res.results]
    return np.roll(out, SHIFT, axis=1)
